# version 01
# baseline (speedup 1.0000x reference)
import numpy as np
import concourse.bass as bass
import concourse.mybir as mybir
from concourse.bass_utils import run_bass_kernel_spmd
from concourse.alu_op_type import AluOpType as ALU
from contextlib import ExitStack

F32 = mybir.dt.float32
BF16 = mybir.dt.bfloat16
AF = mybir.ActivationFunctionType

B, T, D = 4, 8192, 2048
H, DK, CS = 16, 128, 64
NTOK = T // 2
TT = 512
NT_FULL = NTOK // TT
NTB = TT // 128
NCH = TT // CS
NIC = D // 128
HS_SH = 2816
H_E = 1408
NE = 8
EPS_LN = 1e-5
EPS_DEN = 1e-6
SC = 129


class Sem:
    def __init__(self, name):
        self.name, self.val = name, 0

    def inc(self, n=1):
        self.val += n
        return self.val


class StopEmission(Exception):
    pass


class Builder:
    def __init__(self, n_tiles=NT_FULL, n_pre=NT_FULL, debug_taps=(), stop_after_tap=None):
        self.stop_after_tap = stop_after_tap
        self.n_tiles, self.n_pre = n_tiles, n_pre
        self.debug_taps = list(debug_taps)
        self.items = {e: [] for e in ("sync", "gpsimd", "vector", "scalar", "tensor")}
        self.sems = {}
        self.psum_free = [None] * 8
        self.strip_free = [None, None]
        self.strip_idx = 0
        self._sg_guard = None
        self._tapset = set(debug_taps)
        self._tapped = set()

    def _emit(self, eng, fn, waits, inc, inc_n):
        waits = tuple(w for w in waits if w is not None)
        mil = (inc, inc.inc(inc_n)) if inc is not None else None
        self.items[eng].append((waits, fn, inc, inc_n))
        return mil

    def pe(self, fn, waits=(), banks=()):
        w = list(waits) + [self.psum_free[b] for b in banks]
        return self._emit("tensor", fn, w, self.s_pe, 1)

    def dve(self, fn, waits=(), rd_banks=()):
        m = self._emit("vector", fn, waits, self.s_dve, 1)
        for b in rd_banks:
            self.psum_free[b] = m
        return m

    def act(self, fn, waits=(), rd_banks=()):
        m = self._emit("scalar", fn, waits, self.s_act, 1)
        for b in rd_banks:
            self.psum_free[b] = m
        return m

    def pool(self, fn, waits=()):
        return self._emit("gpsimd", fn, waits, self.s_pool, 1)

    def dma(self, queue, out_fn, in_fn, sem, waits=(), **kw):
        def fn(eng):
            return eng.dma_start(out=out_fn(), in_=in_fn(), **kw)
        return self._emit(queue, fn, waits, sem, 16)

    def tap(self, nm, ap_fn, waits):
        if nm not in self._tapset or nm in self._tapped:
            return
        self._tapped.add(nm)
        t = self.nc.dram_tensor("dbg_" + nm, list(ap_fn().shape), ap_fn().dtype,
                                kind="ExternalOutput").ap()
        mil = self.dma("gpsimd", (lambda t=t: t), ap_fn, self.s_out, waits=waits)
        for e in self.items:
            self.items[e].append(((mil,), (lambda eng: eng.nop()), None, 0))
        if self.stop_after_tap == nm:
            raise StopEmission()

    def load_strip(self, nm, kc, col0, cols, extra_waits=()):
        bidx = self.strip_idx % 2
        self.strip_idx += 1
        waits = [self.cache_ready[nm], self.strip_free[bidx]] + list(extra_waits)
        sem = self.s_w0 if bidx == 0 else self.s_w1
        mil = self.dma("sync",
                       (lambda b=bidx, k=kc, c=cols: self.wstrip[:, b, 0:k * c]
                        .rearrange("p (a b) -> p a b", a=k)),
                       (lambda n=nm, k=kc, c0=col0, c=cols: self.w_c[n][:, 0:k, c0:c0 + c]),
                       sem, waits=waits)
        view = self.wstrip[:, bidx, 0:kc * cols].rearrange("p (a b) -> p a b", a=kc)
        return view, mil, bidx

    def build(self):
        n_tiles, n_pre = self.n_tiles, self.n_pre
        ntok, npre = n_tiles * TT, n_pre * TT
        nc = bass.Bass("TRN2", target_bir_lowering=False,
                       detect_race_conditions=False)
        self.nc = nc

        def din(name, shape, dt=F32):
            return nc.dram_tensor(name, shape, dt, kind="ExternalInput").ap()

        self.x_own = din("x_own", [128, NIC, ntok])
        self.x_pre = din("x_pre", [128, NIC, max(npre, TT)])
        w_in = {}
        self.wspec = [("wq", NIC, D), ("wk", NIC, D), ("wv", NIC, D), ("wg", NIC, D),
                      ("wo", NIC, D), ("wsgw", NIC, 2 * HS_SH)] + \
                     [(f"wup{e}", NIC, 2 * H_E) for e in range(NE)] + \
                     [("wdnA", 33, D), ("wdnB", 33, D), ("wdnC", 22, D), ("wdnD", 22, D)]
        for nm, kc, m in self.wspec:
            w_in[nm] = din(nm, [128, kc, m])
        wgate_i = din("wgate", [128, NIC, NE])
        bq_i = din("bq", [128, NIC])
        bo_i = din("bo", [128, NIC])
        brows_i = din("brows", [1, 3 * D])
        gbias_i = din("gbias", [128, NE])
        maskS_i = din("maskS", [128, 1])
        cosq_i = din("cosq", [128, ntok])
        sinq_i = din("sinq", [128, ntok])
        kco_i = din("kcos_o", [128, ntok // 128, 64])
        kso_i = din("ksin_o", [128, ntok // 128, 64])
        kcp_i = din("kcos_p", [128, max(npre, TT) // 128, 64])
        ksp_i = din("ksin_p", [128, max(npre, TT) // 128, 64])
        ident_i = din("ident", [128, 128])
        pswap_i = din("pswap", [128, 128])
        onescol_i = din("onescol", [128, 1])
        onesrow_i = din("onesrow", [1, 128])
        sel8_i = din("sel8", [8, 8 * 128])
        self.y_out = nc.dram_tensor("y", [128, NIC, ntok], F32, kind="ExternalOutput").ap()

        self.w_c = {nm: nc.dram_tensor(nm + "_c", [128, kc, m], BF16, kind="Internal").ap()
                    for nm, kc, m in self.wspec}

        es = ExitStack()
        with es:
            sb = lambda name, shape, dt: es.enter_context(nc.sbuf_tensor(name, shape, dt))
            self.xb = xb = sb("xb", [128, NIC, TT], F32)
            self.lnbuf = lnbuf = sb("lnbuf", [128, NIC, TT], BF16)
            self.Sprev = Sprev = sb("Sprev", [128, H, SC], F32)
            self.Sprev_bf = Sprev_bf = sb("Sprev_bf", [128, H, SC], BF16)
            dSbf = sb("dSbf", [128, 2, SC], BF16)
            bq_s = sb("bq_s", [128, NIC], F32)
            bo_s = sb("bo_s", [128, NIC], F32)
            brow_bf = sb("brow_bf", [1, 3 * D], BF16)
            gbias_s = sb("gbias_s", [128, NE], F32)
            maskS_s = sb("maskS_s", [128, 1], F32)
            ident_bf = sb("ident_bf", [128, 128], BF16)
            ident_f = sb("ident_f", [128, 128], F32)
            pswap_bf = sb("pswap_bf", [128, 128], BF16)
            onescol_bf = sb("onescol_bf", [128, 1], BF16)
            onesrow_bf = sb("onesrow_bf", [1, 128], BF16)
            sel8_bf = sb("sel8_bf", [8, 8 * 128], BF16)
            onesrow_f = sb("onesrow_f", [1, 128], F32)
            gate_bf = sb("gate_bf", [128, NIC, NE], BF16)
            b1_sb = sb("b1_sb", [128, TT], F32)
            b0_sb = sb("b0_sb", [128, TT], F32)
            rows4_t = sb("rows4", [1, 3 * TT], F32)
            cosq_s = sb("cosq_s", [128, TT], BF16)
            sinq_s = sb("sinq_s", [128, TT], BF16)
            kcos_s = sb("kcos_s", [128, NTB, 64], BF16)
            ksin_s = sb("ksin_s", [128, NTB, 64], BF16)
            den_s = sb("den_s", [128, 1], F32)
            rd_s = sb("rd_s", [128, 1], F32)
            aotok = sb("aotok", [128, 2, 128], BF16)
            sfm = sb("sfm", [8, TT], F32)
            stok = sb("stok", [128, NTB, NE], F32)
            r_m1 = sb("r_m1", [128, NTB], F32)
            weT = sb("weT", [8, TT], BF16)
            self.wstrip = sb("wstrip", [128, 2, 8192], BF16)
            ARENA = 42048
            arena = sb("arena", [128, ARENA], BF16)

            def carve(off, shape, dt=BF16):
                n = int(np.prod(shape[1:]))
                if dt == F32:
                    assert off % 2 == 0
                    ap = arena[:, off:off + 2 * n].bitcast(F32)
                    nxt = off + 2 * n
                else:
                    ap = arena[:, off:off + n]
                    nxt = off + n
                if len(shape) > 2:
                    names = [chr(97 + i) for i in range(len(shape) - 1)]
                    r = "p (" + " ".join(names) + ") -> p " + " ".join(names)
                    kw = {names[i]: shape[1 + i] for i in range(len(shape) - 2)}
                    ap = ap.rearrange(r, **kw)
                return ap, nxt

            off = 0
            qbb, off = carve(off, [128, H, TT])
            ktile, off = carve(off, [128, NTB, D])
            k2, off = carve(off, [128, NTB, D])
            v_t, off = carve(off, [128, NTB, H, SC])
            g_t, off = carve(off, [128, NTB, D])
            qhtmp, off = carve(off, [128, 2, TT])
            assert off <= ARENA, f"attn overlay {off}"
            o_f32 = arena[:, 0:2 * NIC * TT].bitcast(F32) \
                .rearrange("p (a b) -> p a b", a=NIC)
            def ao_wr(h, tb):
                return k2[:, tb, h * 128:(h + 1) * 128]
            def ao_rd(ic):
                return k2[:, :, ic * 128:(ic + 1) * 128]
            sqscr_k = arena[:, H * TT:H * TT + NIC * TT].rearrange("p (a b) -> p a b", a=NIC)
            goff = 3 * H * TT + NTB * H * SC
            sqscr_g = arena[:, goff:goff + NIC * TT].rearrange("p (a b) -> p a b", a=NIC)
            off = 0
            h_shh, off = carve(off, [128, 11, TT])
            h_e0, off = carve(off, [128, 11, TT])
            h_e1, off = carve(off, [128, 11, TT])
            sg_e, off = carve(off, [128, 11, TT])
            f_sb, off = carve(off, [128, NIC, TT], F32)
            uw_t, off = carve(off, [128, 2, TT])
            we2, off = carve(off, [128, 2, TT])
            assert off <= ARENA, f"ffn overlay {off}"
            sqscr_ffn = arena[:, 0:NIC * TT].rearrange("p (a b) -> p a b", a=NIC)
            roff = 3 * 11 * TT
            def rcarve(roff, shape):
                n = int(np.prod(shape[1:]))
                ap = arena[:, roff:roff + 2 * n].bitcast(F32)
                if len(shape) == 3:
                    ap = ap.rearrange("p (a b) -> p a b", a=shape[1])
                return ap, roff + 2 * n
            stok2, roff = rcarve(roff, [128, NTB, NE])
            r_t, roff = rcarve(roff, [128, NTB, NE])
            r_w, roff = rcarve(roff, [128, NTB, NE])
            r_m1b, roff = rcarve(roff, [128, NTB])
            r_ws4, roff = rcarve(roff, [128, NTB])
            assert roff <= 4 * 11 * TT

            pb = [es.enter_context(nc.psum_tensor(f"pb{i}", [128, 512], F32)) for i in range(8)]
            handles = {}
            for s in ("s_ld", "s_cast", "s_out", "s_pe", "s_dve", "s_act", "s_pool",
                      "s_w0", "s_w1", "s_x", "s_cd"):
                handles[s] = es.enter_context(nc.semaphore(s))
                setattr(self, s, Sem(s))
            block = es.enter_context(nc.Block())

            cm = {}
            for nm, dst, src in [("bq", lambda: bq_s[:], lambda: bq_i),
                                 ("bo", lambda: bo_s[:], lambda: bo_i),
                                 ("gb", lambda: gbias_s[:], lambda: gbias_i),
                                 ("mk", lambda: maskS_s[:], lambda: maskS_i),
                                 ("idf", lambda: ident_f[:], lambda: ident_i),
                                 ("orf", lambda: onesrow_f[:], lambda: onesrow_i)]:
                cm[nm] = self.dma("sync", dst, src, self.s_ld)
            for nm, dst, src in [("idb", lambda: ident_bf[:], lambda: ident_i),
                                 ("psw", lambda: pswap_bf[:], lambda: pswap_i),
                                 ("ocb", lambda: onescol_bf[:], lambda: onescol_i),
                                 ("orb", lambda: onesrow_bf[:], lambda: onesrow_i),
                                 ("brw", lambda: brow_bf[:], lambda: brows_i),
                                 ("sel8", lambda: sel8_bf[:], lambda: sel8_i),
                                 ("gt", lambda: gate_bf[:], lambda: wgate_i)]:
                cm[nm] = self.dma("gpsimd", dst, src, self.s_cast)
            ld_total = self.s_ld.val
            for k in list(cm.keys()):
                sem0, _v = cm[k]
                if sem0 is self.s_ld:
                    cm[k] = (self.s_ld, ld_total)
            cast_total0 = self.s_cast.val
            self.items["gpsimd"].append((
                ((self.s_cast, cast_total0),),
                lambda g: g.engine_nop(), self.s_cd, 1))
            mcd0 = (self.s_cd, self.s_cd.inc(1) - 0)
            for k in list(cm.keys()):
                sem0, _v = cm[k]
                if sem0 is self.s_cast:
                    cm[k] = mcd0
            self.cm = cm
            mz = self.dve(lambda v: v.memset(Sprev[:], 0.0))
            mzb = self.dve(lambda v: v.memset(Sprev_bf[:], 0.0))

            self.cache_ready = {}
            order = ["wk", "wv", "wq", "wg", "wo", "wsgw"] + [f"wup{e}" for e in range(2)] + \
                    ["wdnA"] + [f"wup{e}" for e in range(2, NE)] + ["wdnB", "wdnC", "wdnD"]
            for nm in order:
                self.dma("gpsimd", (lambda n=nm: self.w_c[n]), (lambda n=nm: w_in[n]),
                         self.s_cast, max_dma_last_dim=8192)
                self.items["gpsimd"].append((
                    ((self.s_cast, self.s_cast.val),),
                    lambda g: g.engine_nop(), self.s_cd, 1))
                self.cache_ready[nm] = (self.s_cd, self.s_cd.inc(1))

            def b1rep():
                return b1_sb[:, :].unsqueeze(1).broadcast_to((128, NIC, TT))

            def b0rep():
                return b0_sb[:, :].unsqueeze(1).broadcast_to((128, NIC, TT))

            def emit_ln(src_f32, sq_scr, wstats):
                mc = self.dve(lambda v: v.tensor_copy(out=lnbuf[:], in_=src_f32), waits=wstats)
                msq = self.pool(lambda g: g.tensor_tensor(out=sq_scr, in0=src_f32, in1=src_f32,
                                                          op=ALU.mult), waits=wstats)
                def statmm(bank_i, rhs, wl):
                    def fn(t):
                        for ic in range(NIC):
                            mm = t.matmul(pb[bank_i][0:1, 0:TT], onescol_bf[:],
                                          rhs[:, ic, :], start=(ic == 0), stop=(ic == NIC - 1))
                        return mm
                    return self.pe(fn, waits=wl, banks=(bank_i,))
                m1 = statmm(6, lnbuf, [mc, cm["ocb"]])
                m2 = statmm(7, sq_scr, [msq])
                def smalls(v):
                    v.tensor_scalar(out=rows4_t[0:1, 0 * TT:(0 + 1) * TT], in0=pb[6][0:1, 0:TT], scalar1=1.0 / D,
                                    scalar2=None, op0=ALU.mult)
                    v.tensor_scalar(out=rows4_t[0:1, 1 * TT:(1 + 1) * TT], in0=pb[7][0:1, 0:TT], scalar1=1.0 / D,
                                    scalar2=None, op0=ALU.mult)
                    v.tensor_tensor(out=rows4_t[0:1, 2 * TT:(2 + 1) * TT], in0=rows4_t[0:1, 0 * TT:(0 + 1) * TT], in1=rows4_t[0:1, 0 * TT:(0 + 1) * TT],
                                    op=ALU.mult)
                    v.tensor_tensor(out=rows4_t[0:1, 1 * TT:(1 + 1) * TT], in0=rows4_t[0:1, 1 * TT:(1 + 1) * TT], in1=rows4_t[0:1, 2 * TT:(2 + 1) * TT],
                                    op=ALU.subtract)
                    return v.tensor_scalar(out=rows4_t[0:1, 1 * TT:(1 + 1) * TT], in0=rows4_t[0:1, 1 * TT:(1 + 1) * TT], scalar1=EPS_LN,
                                           scalar2=None, op0=ALU.add)
                ms = self.dve(smalls, waits=[m1, m2], rd_banks=(6, 7))
                ml = self.act(lambda s: s.activation(out=rows4_t[0:1, 1 * TT:(1 + 1) * TT], in_=rows4_t[0:1, 1 * TT:(1 + 1) * TT],
                                                     func=AF.Ln), waits=[ms])
                me = self.act(lambda s: s.activation(out=rows4_t[0:1, 1 * TT:(1 + 1) * TT], in_=rows4_t[0:1, 1 * TT:(1 + 1) * TT],
                                                     func=AF.Exp, scale=-0.5), waits=[ml])
                mn = self.dve(lambda v: (v.tensor_tensor(out=rows4_t[0:1, 2 * TT:(2 + 1) * TT], in0=rows4_t[0:1, 0 * TT:(0 + 1) * TT],
                                                         in1=rows4_t[0:1, 1 * TT:(1 + 1) * TT], op=ALU.mult),
                                         v.tensor_scalar(out=rows4_t[0:1, 2 * TT:(2 + 1) * TT], in0=rows4_t[0:1, 2 * TT:(2 + 1) * TT],
                                                         scalar1=-1.0, scalar2=None,
                                                         op0=ALU.mult))[-1], waits=[me])
                mb1 = self.pe(lambda t: t.matmul(pb[6][:, 0:TT], onesrow_f[:], rows4_t[0:1, 1 * TT:(1 + 1) * TT],
                                                 start=True, stop=True),
                              waits=[mn, cm["orf"]], banks=(6,))
                mb0 = self.pe(lambda t: t.matmul(pb[7][:, 0:TT], onesrow_f[:], rows4_t[0:1, 2 * TT:(2 + 1) * TT],
                                                 start=True, stop=True), waits=[mn], banks=(7,))
                me1 = self.dve(lambda v: v.tensor_copy(out=b1_sb[:], in_=pb[6][:, 0:TT]),
                               waits=[mb1], rd_banks=(6,))
                me0 = self.dve(lambda v: v.tensor_copy(out=b0_sb[:], in_=pb[7][:, 0:TT]),
                               waits=[mb0], rd_banks=(7,))
                return me0, me1

            def emit_fm_proj(wname, out_cols, rhs_ready, evac_fn, extra_mm_waits=()):
                evs = []
                n_oc = out_cols // 128
                oc = 0
                while oc < n_oc:
                    cols = min(512, (n_oc - oc) * 128)
                    wv, wm, bidx = self.load_strip(wname, NIC, oc * 128, cols)
                    last_mm = None
                    for j in range(cols // 128):
                        bank_i = (oc + j) % 2
                        wl = [wm, rhs_ready] + list(extra_mm_waits)
                        def fn(t, wv=wv, j=j, bank_i=bank_i):
                            for ic in range(NIC):
                                mm = t.matmul(pb[bank_i][:, 0:512],
                                              wv[:, ic, j * 128:(j + 1) * 128],
                                              lnbuf[:, ic, :],
                                              start=(ic == 0), stop=(ic == NIC - 1))
                            return mm
                        mm_m = self.pe(fn, waits=wl, banks=(bank_i,))
                        last_mm = mm_m
                        evs.append(evac_fn(oc + j, bank_i, mm_m))
                    self.strip_free[bidx] = last_mm
                    oc += cols // 128
                return evs

            def emit_tokproj(wname, brow_idx, rhs_ready, evac_fn):
                evs = []
                for dc in range(4):
                    wv, wm, bidx = self.load_strip(wname, NIC, dc * 512, 512)
                    last_mm = None
                    for tb in range(NTB):
                        bank_i = (dc * NTB + tb) % 2
                        def fn(t, wv=wv, tb=tb, bank_i=bank_i, dc=dc):
                            t.matmul(pb[bank_i][:, 0:512], onesrow_bf[:],
                                     brow_bf[0:1, brow_idx * D + dc * 512:brow_idx * D + (dc + 1) * 512],
                                     start=True, stop=False)
                            for ic in range(NIC):
                                mm = t.matmul(pb[bank_i][:, 0:512],
                                              lnbuf[:, ic, tb * 128:(tb + 1) * 128],
                                              wv[:, ic, :], start=False, stop=(ic == NIC - 1))
                            return mm
                        mm_m = self.pe(fn, waits=[wm, rhs_ready, cm["orb"], cm["brw"]],
                                       banks=(bank_i,))
                        last_mm = mm_m
                        evs.append(evac_fn(tb, dc, bank_i, mm_m))
                    self.strip_free[bidx] = last_mm
                return evs

            def emit_scan(pre, kv_ready, g_ready, q_ready):
                state_free = [self.psum_free[4], self.psum_free[5]]
                nonloc = {}
                sprev_last = None
                dve_last = None
                prev_mr = None
                prev_me = None
                for h in range(H):
                    sbank = pb[4 + h % 2]
                    sbi = 4 + h % 2
                    last_ev = None
                    last_num = None
                    mu_ = None
                    for c in range(NCH):
                        tb, cpar = c // 2, (c % 2) * 64
                        if c == 0:
                            wl = list(kv_ready) + [state_free[h % 2]]
                        else:
                            wl = [last_ev]
                        mu_ = self.pe(lambda t, tb=tb, cp=cpar, h=h, c=c, sbank=sbank:
                                      t.matmul(sbank[:, 0:SC],
                                               k2[cp:cp + 64, tb, h * 128:(h + 1) * 128],
                                               v_t[cp:cp + 64, tb, h, :],
                                               start=(c == 0), stop=True,
                                               skip_group_check=True),
                                      waits=wl)
                        wde = [mu_]
                        if c >= 2:
                            wde.append(nonloc.get(("num", h, c - 2)))
                        else:
                            wde.append(nonloc.get(("num", h - 1, 6 + (c % 2))))
                        last_ev = self.dve(lambda v, c=c, sbank=sbank: v.tensor_copy(
                            out=dSbf[:, c % 2, :], in_=sbank[:, 0:SC]), waits=wde)
                        if pre:
                            continue
                        nbi = 2 + (c // 2) % 2
                        def numfn(t, h=h, c=c, nbi=nbi, cp=cpar):
                            t.matmul(pb[nbi][cp:cp + 64, 0:SC],
                                     qbb[:, h, c * CS:(c + 1) * CS], dSbf[:, c % 2, :],
                                     start=True, stop=False)
                            return t.matmul(pb[nbi][cp:cp + 64, 0:SC],
                                            qbb[:, h, c * CS:(c + 1) * CS], Sprev_bf[:, h, :],
                                            start=False, stop=True)
                        bank_wait = (nbi,) if c % 2 == 0 else ()
                        if h == 0 and c == 0:
                            self.tap("ds00", lambda: dSbf[:, 0, :], [last_ev])
                        mn_ = self.pe(numfn, waits=[last_ev, q_ready], banks=bank_wait)
                        nonloc[("num", h, c)] = mn_
                        last_num = mn_
                        if c % 2 == 1:
                            tb2 = c // 2
                            md_ = self.dve(lambda v, nbi=nbi: v.tensor_scalar(
                                out=den_s[:], in0=pb[nbi][:, 128:129],
                                scalar1=EPS_DEN, scalar2=None, op0=ALU.add),
                                waits=[mn_, prev_mr])
                            mr_ = self.act(lambda s: (s.activation(
                                out=rd_s[:], in_=den_s[:], func=AF.Ln),
                                s.activation(out=rd_s[:], in_=rd_s[:], func=AF.Exp,
                                             scale=-1.0))[-1], waits=[md_, prev_me])
                            prev_mr = mr_
                            def epi(v, nbi=nbi, tb2=tb2, h=h):
                                v.tensor_scalar(out=aotok[:, h % 2, :], in0=pb[nbi][:, 0:128],
                                                scalar1=rd_s[:], scalar2=None, op0=ALU.mult)
                                return v.tensor_tensor(out=aotok[:, h % 2, :],
                                                       in0=aotok[:, h % 2, :],
                                                       in1=g_t[:, tb2, h * 128:(h + 1) * 128],
                                                       op=ALU.mult)
                            wepi = [mn_, mr_] + list(g_ready)
                            if tb2 >= 1:
                                wepi.append(nonloc.get(("mt", h, tb2 - 1)))
                            else:
                                wepi.append(nonloc.get(("mt", h - 2, 3)))
                            me_ = self.dve(epi, waits=wepi, rd_banks=(nbi,))
                            if h == 0 and c == 1:
                                self.tap("ao00", lambda: aotok[:, 0, :], [me_])
                                self.tap("den00", lambda: den_s[:], [me_])
                                self.tap("rd00", lambda: rd_s[:], [me_])
                            tbi = (c // 2) % 2
                            mt_ = self.pe(lambda t, h=h, tbi=tbi: t.transpose(
                                pb[tbi][:, 0:64].bitcast(BF16), aotok[:, h % 2, :],
                                ident_bf[:]), waits=[me_, cm["idb"]], banks=(tbi,))
                            nonloc[("mt", h, tb2)] = mt_
                            mte = self.dve(lambda v, tb2=tb2, h=h, tbi=tbi: v.tensor_copy(
                                out=ao_wr(h, tb2), in_=pb[tbi][:, 0:64].bitcast(BF16)),
                                waits=[mt_], rd_banks=(tbi,))
                            dve_last = mte
                    wfold = [mu_] if pre else [mu_, last_num]
                    msp = self.dve(lambda v, h=h, sbank=sbank: v.tensor_tensor(
                        out=Sprev[:, h, :], in0=Sprev[:, h, :], in1=sbank[:, 0:SC], op=ALU.add),
                        waits=wfold, rd_banks=(sbi,))
                    mspb = self.dve(lambda v, h=h: v.tensor_copy(out=Sprev_bf[:, h, :],
                                                                 in_=Sprev[:, h, :]), waits=[msp])
                    state_free[h % 2] = mspb
                    sprev_last = mspb
                    if pre:
                        dve_last = mspb
                return sprev_last, dve_last

            st = dict(pe_end=None, dve_end=None, out_m=None, rope_free=None,
                      first_tile=True, sprev_ready=[mz, mzb])

            def emit_tile(ti, pre, x_src, kct, kst):
                t0 = ti * TT
                mx = self.dma("sync", lambda: xb[:], lambda: x_src[:, :, t0:t0 + TT],
                              self.s_x, waits=[st["dve_end"], st["out_m"]])
                self.dma("gpsimd", lambda: kcos_s[:],
                         lambda: kct[:, ti * NTB:(ti + 1) * NTB, :], self.s_cast,
                         waits=[st["rope_free"]])
                mks = self.dma("gpsimd", lambda: ksin_s[:],
                               lambda: kst[:, ti * NTB:(ti + 1) * NTB, :], self.s_cast)
                mkc = (self.s_cast, self.s_cast.val)
                if not pre:
                    self.dma("gpsimd", lambda: cosq_s[:], lambda: cosq_i[:, t0:t0 + TT],
                             self.s_cast)
                    msq_ = self.dma("gpsimd", lambda: sinq_s[:], lambda: sinq_i[:, t0:t0 + TT],
                                    self.s_cast)
                    mcq = (self.s_cast, self.s_cast.val)
                    mks = mcq

                wstats = [mx, st["pe_end"], st["dve_end"]]
                me0, me1 = emit_ln(xb[:, :, :], sqscr_k, wstats)
                mn1 = self.dve(lambda v: v.tensor_tensor(out=lnbuf[:], in0=xb[:], in1=b1rep(),
                                                         op=ALU.mult), waits=[me1])
                ln_ready = self.dve(lambda v: v.tensor_tensor(out=lnbuf[:], in0=lnbuf[:],
                                                              in1=b0rep(), op=ALU.add),
                                    waits=[me0, mn1])
                if not pre:
                    self.tap("ln0", lambda: lnbuf[:], [ln_ready])
                else:
                    self.tap("pln0", lambda: lnbuf[:], [ln_ready])

                def k_write(tb, dc, bank_i, mm_m):
                    return self.dve(lambda v, tb=tb, dc=dc, b=bank_i: v.tensor_copy(
                        out=ktile[:, tb, dc * 512:(dc + 1) * 512], in_=pb[b][:, 0:512]),
                        waits=[mm_m], rd_banks=(bank_i,))
                k_evs = emit_tokproj("wk", 0, ln_ready, k_write)
                if not pre:
                    self.tap("kraw", lambda: ktile[:], k_evs[-1:])
                kcr = kcos_s[:, :, :].unsqueeze(2).broadcast_to((128, NTB, H, 64))
                ksr = ksin_s[:, :, :].unsqueeze(2).broadcast_to((128, NTB, H, 64))
                def iv(ap, par):
                    return ap.rearrange("p a (h f two) -> p a h f two", h=H, two=2)[:, :, :, :, par]
                kev, kod = iv(ktile, 0), iv(ktile, 1)
                k2ev, k2od = iv(k2, 0), iv(k2, 1)
                scr = iv(g_t, 0)
                mr1 = self.dve(lambda v: v.tensor_tensor(out=k2ev, in0=kev, in1=kcr, op=ALU.mult),
                               waits=k_evs + [mkc, mks])
                mr2 = self.dve(lambda v: v.tensor_tensor(out=k2od, in0=kod, in1=kcr, op=ALU.mult))
                mr3 = self.dve(lambda v: v.tensor_tensor(out=scr, in0=kod, in1=ksr, op=ALU.mult))
                mr4 = self.dve(lambda v: v.tensor_tensor(out=k2ev, in0=k2ev, in1=scr,
                                                         op=ALU.subtract))
                mr5 = self.dve(lambda v: v.tensor_tensor(out=scr, in0=kev, in1=ksr, op=ALU.mult))
                mr6 = self.dve(lambda v: v.tensor_tensor(out=k2od, in0=k2od, in1=scr, op=ALU.add))
                if not pre:
                    self.tap("krope", lambda: k2[:], [mr6])
                mr7 = self.dve(lambda v: v.tensor_scalar(out=ktile, in0=k2, scalar1=0.0,
                                                         scalar2=None, op0=ALU.min), waits=[mr6])
                mr8 = self.act(lambda s: s.activation(out=ktile, in_=ktile, func=AF.Exp),
                               waits=[mr7])
                k_done = self.dve(lambda v: v.scalar_tensor_tensor(
                    out=k2, in0=k2, scalar=0.0, op0=ALU.max, in1=ktile, op1=ALU.add),
                    waits=[mr8])
                rope_free_k = mr5

                mones = self.dve(lambda v: v.memset(v_t[:, :, :, 128:129], 1.0),
                                 waits=[st["pe_end"], st["dve_end"]])
                def v_write(tb, dc, bank_i, mm_m):
                    return self.dve(lambda v, tb=tb, dc=dc, b=bank_i: v.tensor_copy(
                        out=v_t[:, tb, dc * 4:(dc + 1) * 4, 0:128],
                        in_=pb[b][:, 0:512].rearrange("p (a c) -> p a c", a=4)),
                        waits=[mm_m], rd_banks=(bank_i,))
                v_evs = emit_tokproj("wv", 1, ln_ready, v_write)

                g_evs, q_done, rope_free_q = [], None, None
                if not pre:
                    def g_write(tb, dc, bank_i, mm_m):
                        return self.act(lambda s, tb=tb, dc=dc, b=bank_i: s.activation(
                            out=g_t[:, tb, dc * 512:(dc + 1) * 512], in_=pb[b][:, 0:512],
                            func=AF.Sigmoid), waits=[mm_m, mr5], rd_banks=(bank_i,))
                    g_evs = emit_tokproj("wg", 2, ln_ready, g_write)

                    q_evs = []
                    def q_write(oc, bank_i, mm_m):
                        return self.dve(lambda v, oc=oc, b=bank_i: v.tensor_scalar(
                            out=qbb[:, oc, :], in0=pb[b][:, 0:512], scalar1=bq_s[:, oc:oc + 1],
                            scalar2=None, op0=ALU.add), waits=[mm_m, cm["bq"]],
                            rd_banks=(bank_i,))
                    q_evs = emit_fm_proj("wq", D, ln_ready, q_write)
                    q_head_done = []
                    for h in range(H):
                        sbi2 = 2 + h % 2
                        wq1 = [q_evs[h], mcq, msq_]
                        mqc = self.dve(lambda v, h=h: v.tensor_tensor(
                            out=qhtmp[:, 0, :], in0=qbb[:, h, :], in1=cosq_s[:], op=ALU.mult),
                            waits=wq1)
                        msw = self.pe(lambda t, h=h, sbi2=sbi2: t.matmul(
                            pb[sbi2][:, 0:512], pswap_bf[:], qbb[:, h, :], start=True, stop=True),
                            waits=[q_evs[h], cm["psw"]], banks=(sbi2,))
                        msin = self.dve(lambda v, sbi2=sbi2: v.tensor_tensor(
                            out=qhtmp[:, 1, :], in0=pb[sbi2][:, 0:512], in1=sinq_s[:],
                            op=ALU.mult), waits=[msw], rd_banks=(sbi2,))
                        mqr = self.dve(lambda v, h=h: v.tensor_tensor(
                            out=qbb[:, h, :], in0=qhtmp[:, 0, :], in1=qhtmp[:, 1, :], op=ALU.add),
                            waits=[msin, mqc])
                        mmin = self.dve(lambda v, h=h: v.tensor_scalar(
                            out=qhtmp[:, 0, :], in0=qbb[:, h, :], scalar1=0.0, scalar2=None,
                            op0=ALU.min))
                        mex = self.act(lambda s: s.activation(out=qhtmp[:, 1, :],
                                                              in_=qhtmp[:, 0, :], func=AF.Exp),
                                       waits=[mmin])
                        mfin = self.dve(lambda v, h=h: v.scalar_tensor_tensor(
                            out=qbb[:, h, :], in0=qbb[:, h, :], scalar=0.0, op0=ALU.max,
                            in1=qhtmp[:, 1, :], op1=ALU.add), waits=[mex])
                        q_head_done.append(mfin)
                    q_done = q_head_done[-1]
                    rope_free_q = q_head_done[-1]

                if pre:
                    self.tap("pk", lambda: k2[:], [k_done])
                    self.tap("pv", lambda: v_t[:], [v_evs[-1], mones])
                if not pre:
                    self.tap("kf", lambda: k2[:], [k_done])
                    self.tap("vt", lambda: v_t[:], [v_evs[-1], mones])
                    if g_evs:
                        self.tap("gt", lambda: g_t[:], [g_evs[-1]])
                    if q_done:
                        self.tap("qf", lambda: qbb[:], [q_done])
                kv_ready = [k_done, v_evs[-1], mones]
                g_ready = [g_evs[-1]] if g_evs else []
                sprev_last, scan_dve_last = emit_scan(pre, kv_ready, g_ready, q_done)
                st["rope_free"] = rope_free_q or rope_free_k
                if not pre:
                    self.tap("sprev", lambda: Sprev[:], [sprev_last])
                    self.tap("ao", lambda: k2[:], [scan_dve_last])

                if pre:
                    self.tap("pscan", lambda: Sprev[:], [sprev_last])
                    st["pe_end"] = None
                    st["dve_end"] = scan_dve_last
                    st["sprev_ready"] = [sprev_last]
                    return

                def o_write(oc, bank_i, mm_m):
                    return self.dve(lambda v, oc=oc, b=bank_i: v.tensor_scalar(
                        out=o_f32[:, oc, :], in0=pb[b][:, 0:512], scalar1=bo_s[:, oc:oc + 1],
                        scalar2=None, op0=ALU.add), waits=[mm_m, cm["bo"]], rd_banks=(bank_i,))
                def emit_wo():
                    evs = []
                    n_oc = NIC
                    oc = 0
                    while oc < n_oc:
                        wv, wm, bidx = self.load_strip("wo", NIC, oc * 128, 512)
                        last_mm = None
                        for j in range(4):
                            bank_i = (oc + j) % 2
                            def fn(t, wv=wv, j=j, bank_i=bank_i):
                                for ic in range(NIC):
                                    mm = t.matmul(pb[bank_i][:, 0:512],
                                                  wv[:, ic, j * 128:(j + 1) * 128],
                                                  ao_rd(ic),
                                                  start=(ic == 0), stop=(ic == NIC - 1))
                                return mm
                            mm_m = self.pe(fn, waits=[wm, scan_dve_last], banks=(bank_i,))
                            last_mm = mm_m
                            evs.append(o_write(oc + j, bank_i, mm_m))
                        self.strip_free[bidx] = last_mm
                        oc += 4
                    return evs
                o_evs = emit_wo()

                self.tap("o", lambda: o_f32, [o_evs[-1]])
                me0, me1 = emit_ln(o_f32, sqscr_g, [o_evs[-1]])
                mm1 = self.dve(lambda v: v.tensor_tensor(out=lnbuf[:], in0=o_f32, in1=b1rep(),
                                                         op=ALU.mult), waits=[me1])
                mm2 = self.dve(lambda v: v.tensor_tensor(out=lnbuf[:], in0=lnbuf[:], in1=b0rep(),
                                                         op=ALU.add), waits=[me0])
                mx1 = self.dve(lambda v: v.tensor_tensor(out=xb[:], in0=xb[:], in1=lnbuf[:],
                                                         op=ALU.add))

                self.tap("x1", lambda: xb[:], [mx1])
                me0b, me1b = emit_ln(xb[:, :, :], sqscr_g, [mx1])
                mn1b = self.dve(lambda v: v.tensor_tensor(out=lnbuf[:], in0=xb[:], in1=b1rep(),
                                                          op=ALU.mult), waits=[me1b])
                ln1_ready = self.dve(lambda v: v.tensor_tensor(out=lnbuf[:], in0=lnbuf[:],
                                                               in1=b0rep(), op=ALU.add),
                                     waits=[me0b])

                self.tap("ln1", lambda: lnbuf[:], [ln1_ready])
                def gatefn(t):
                    for ic in range(NIC):
                        mm = t.matmul(pb[2][0:8, 0:TT], gate_bf[:, ic, :], lnbuf[:, ic, :],
                                      start=(ic == 0), stop=(ic == NIC - 1))
                    return mm
                mgate = self.pe(gatefn, waits=[ln1_ready, cm["gt"]], banks=(2,))
                msig = self.act(lambda s: s.activation(out=sfm[:], in_=pb[2][0:8, 0:TT],
                                                       func=AF.Sigmoid), waits=[mgate],
                                rd_banks=(2,))
                tr_free_r = None
                mev = None
                for tb in range(NTB):
                    wt = [msig, cm["idf"]]
                    if tr_free_r is not None:
                        wt.append(tr_free_r)
                    mtr = self.pe(lambda t, tb=tb: t.transpose(
                        pb[3][0:128, 0:8], sfm[0:8, tb * 128:(tb + 1) * 128], ident_f[0:8, 0:8]),
                        waits=wt, banks=(3,))
                    mev = self.dve(lambda v, tb=tb: v.tensor_copy(out=stok[:, tb, :],
                                                                  in_=pb[3][0:128, 0:8]),
                                   waits=[mtr], rd_banks=(3,))
                    tr_free_r = mev
                mpc = self.pool(lambda g: g.tensor_copy(out=stok2[:], in_=stok[:]),
                                waits=[mev])
                m1_ = self.dve(lambda v: [v.tensor_tensor(
                    out=r_t[:, tb, :], in0=stok2[:, tb, :], in1=gbias_s[:], op=ALU.add)
                    for tb in range(NTB)][-1], waits=[mpc, cm["gb"]])
                m2_ = self.dve(lambda v: v.tensor_reduce(
                    out=r_m1[:], in_=r_t[:], axis=mybir.AxisListType.X, op=ALU.max),
                    waits=[m1_])
                m2b = self.pool(lambda g: g.tensor_copy(out=r_m1b[:], in_=r_m1[:]),
                                waits=[m2_])
                m3_ = self.dve(lambda v: [v.tensor_scalar(
                    out=r_w[:, tb, :], in0=r_t[:, tb, :], scalar1=r_m1b[:, tb:tb + 1],
                    scalar2=None, op0=ALU.is_ge) for tb in range(NTB)][-1], waits=[m2b])
                m4_ = self.dve(lambda v: [v.scalar_tensor_tensor(
                    out=r_w[:, tb, :], in0=r_w[:, tb, :], scalar=-1e30, op0=ALU.mult,
                    in1=r_t[:, tb, :], op1=ALU.add) for tb in range(NTB)][-1], waits=[m3_])
                m5_ = self.dve(lambda v: v.tensor_reduce(
                    out=r_m1[:], in_=r_w[:], axis=mybir.AxisListType.X, op=ALU.max),
                    waits=[m4_])
                m5b = self.pool(lambda g: g.tensor_copy(out=r_m1b[:], in_=r_m1[:]),
                                waits=[m5_])
                m6_ = self.dve(lambda v: [v.tensor_scalar(
                    out=r_w[:, tb, :], in0=r_t[:, tb, :], scalar1=r_m1b[:, tb:tb + 1],
                    scalar2=None, op0=ALU.is_ge) for tb in range(NTB)][-1], waits=[m5b])
                m7_ = self.dve(lambda v: [v.tensor_tensor(
                    out=r_w[:, tb, :], in0=r_w[:, tb, :], in1=stok2[:, tb, :], op=ALU.mult)
                    for tb in range(NTB)][-1], waits=[m6_])
                m8_ = self.dve(lambda v: v.tensor_reduce(
                    out=r_ws4[:], in_=r_w[:], axis=mybir.AxisListType.X, op=ALU.add),
                    waits=[m7_])
                mrr = self.act(lambda s: (s.activation(out=r_ws4[:], in_=r_ws4[:], func=AF.Ln),
                                          s.activation(out=r_ws4[:], in_=r_ws4[:], func=AF.Exp,
                                                       scale=-1.0))[-1], waits=[m8_])
                route_done = None
                tr_free_w = None
                for tb in range(NTB):
                    mwe_ = self.dve(lambda v, tb=tb: v.tensor_scalar(
                        out=stok[:, tb, :], in0=r_w[:, tb, :], scalar1=r_ws4[:, tb:tb + 1],
                        scalar2=None, op0=ALU.mult), waits=[mrr])
                    mtw = self.pe(lambda t, tb=tb: t.transpose(
                        pb[3][0:8, 0:128], stok[:, tb, :], ident_f[:]),
                        waits=[mwe_] + ([tr_free_w] if tr_free_w else []), banks=(3,))
                    mevw = self.dve(lambda v, tb=tb: v.tensor_copy(
                        out=weT[0:8, tb * 128:(tb + 1) * 128], in_=pb[3][0:8, 0:128]),
                        waits=[mtw], rd_banks=(3,))
                    tr_free_w = mevw
                    route_done = mevw
                self.tap("sfm", lambda: sfm[:], [msig])
                self.tap("we", lambda: stok[:], [route_done])
                uw_guard = [None, None]

                def sh_evac_half(p):
                    def evac(j, oc, bank_i, mm_m):
                        if j < 11:
                            return self.dve(lambda v, j=j, b=bank_i: v.tensor_copy(
                                out=h_shh[:, j, :], in_=pb[b][:, 0:512]), waits=[mm_m],
                                rd_banks=(bank_i,))
                        sl = j % 2
                        ms_ = self.act(lambda s, b=bank_i, sl=sl: s.activation(
                            out=uw_t[:, sl, :], in_=pb[b][:, 0:512], func=AF.Silu),
                            waits=[mm_m, uw_guard[sl]], rd_banks=(bank_i,))
                        mr = self.dve(lambda v, j=j, sl=sl: v.tensor_tensor(
                            out=h_shh[:, j - 11, :], in0=h_shh[:, j - 11, :],
                            in1=uw_t[:, sl, :], op=ALU.mult), waits=[ms_])
                        uw_guard[sl] = mr
                        return mr
                    return evac

                def make_e_evac(eidx, hbuf, we_ready):
                    def evac(j, oc, bank_i, mm_m):
                        if j < 11:
                            m = self.act(lambda s, j=j, b=bank_i: s.activation(
                                out=sg_e[:, j, :], in_=pb[b][:, 0:512], func=AF.Silu),
                                waits=[mm_m, self._sg_guard], rd_banks=(bank_i,))
                            return m
                        sl = j % 2
                        mu = self.dve(lambda v, b=bank_i, e=eidx, sl=sl: v.tensor_tensor(
                            out=uw_t[:, sl, :], in0=pb[b][:, 0:512], in1=we2[:, e % 2, :],
                            op=ALU.mult), waits=[mm_m, we_ready, uw_guard[sl]],
                            rd_banks=(bank_i,))
                        mh = self.dve(lambda v, j=j, hb=hbuf, sl=sl: v.tensor_tensor(
                            out=hb[:, j - 11, :], in0=sg_e[:, j - 11, :], in1=uw_t[:, sl, :],
                            op=ALU.mult), waits=[mu])
                        uw_guard[sl] = mh
                        if j == 21:
                            self._sg_guard = mh
                        return mh
                    return evac

                def emit_cols_proj(wname, oc_list, rhs_ready, evac_fn, extra=()):
                    evs = []
                    pos = 0
                    while pos < len(oc_list):
                        run = [oc_list[pos]]
                        while len(run) < 4 and pos + len(run) < len(oc_list) and \
                                oc_list[pos + len(run)] == run[-1] + 1:
                            run.append(oc_list[pos + len(run)])
                        wv, wm, bidx = self.load_strip(wname, NIC, run[0] * 128, len(run) * 128)
                        last_mm = None
                        for j, oc in enumerate(run):
                            bank_i = (pos + j) % 2
                            wl = [wm, rhs_ready] + list(extra)
                            def fn(t, wv=wv, j=j, bank_i=bank_i):
                                for ic in range(NIC):
                                    mm = t.matmul(pb[bank_i][:, 0:512],
                                                  wv[:, ic, j * 128:(j + 1) * 128],
                                                  lnbuf[:, ic, :],
                                                  start=(ic == 0), stop=(ic == NIC - 1))
                                return mm
                            mm_m = self.pe(fn, waits=wl, banks=(bank_i,))
                            last_mm = mm_m
                            evs.append(evac_fn(pos + j, oc, bank_i, mm_m))
                        self.strip_free[bidx] = last_mm
                        pos += len(run)
                    return evs

                groups = [("wdnA", 0, [(0, h_e0), (1, h_e1)]),
                          ("wdnB", 1, [(2, h_e0), (3, h_e1)]),
                          ("wdnC", None, [(4, h_e0), (5, h_e1)]),
                          ("wdnD", None, [(6, h_e0), (7, h_e1)])]
                f_last = None
                hbuf_free = {}
                for gi, (dnname, shpass, emems) in enumerate(groups):
                    we_ready = None
                    for eidx, _hb in emems:
                        bi = 2 + eidx % 2
                        mwe = self.pe(lambda t, e=eidx, bi=bi: t.matmul(
                            pb[bi][:, 0:TT], sel8_bf[:, e * 128:(e + 1) * 128], weT[0:8, :],
                            start=True, stop=True), waits=[route_done, cm["sel8"]],
                            banks=(bi,))
                        we_ready = self.dve(lambda v, e=eidx, bi=bi: v.tensor_copy(
                            out=we2[:, e % 2, :], in_=pb[bi][:, 0:TT]),
                            waits=[mwe, hbuf_free.get("we")], rd_banks=(bi,))
                    h_ready = []
                    if shpass is not None:
                        ocs = list(range(shpass * 11, shpass * 11 + 11)) + \
                              list(range(22 + shpass * 11, 22 + shpass * 11 + 11))
                        evs = emit_cols_proj("wsgw", ocs, ln1_ready, sh_evac_half(shpass),
                                             extra=[hbuf_free.get(id(h_shh))])
                        h_ready.append(evs[-1])
                    for eidx, hbuf in emems:
                        evs = emit_cols_proj(f"wup{eidx}", list(range(22)), ln1_ready,
                                             make_e_evac(eidx, hbuf, we_ready),
                                             extra=[hbuf_free.get(id(hbuf))])
                        h_ready.append(evs[-1])
                    kc = 33 if shpass is not None else 22
                    rows = ([(h_shh, 0, 11)] if shpass is not None else []) + \
                           [(emems[0][1], kc - 22, 11), (emems[1][1], kc - 11, 11)]
                    down_last = None
                    for oc in range(NIC):
                        wv, wm, bidx = self.load_strip(dnname, kc, oc * 128, 128)
                        bank_i = oc % 2
                        def fn(t, wv=wv, rows=rows, bank_i=bank_i):
                            n_tot = sum(nr for _, _, nr in rows)
                            i = 0
                            for hb, r0, nr in rows:
                                for r in range(nr):
                                    mm = t.matmul(pb[bank_i][:, 0:512], wv[:, r0 + r, :],
                                                  hb[:, r, :], start=(i == 0),
                                                  stop=(i == n_tot - 1))
                                    i += 1
                            return mm
                        mm_m = self.pe(fn, waits=[wm] + h_ready, banks=(bank_i,))
                        down_last = mm_m
                        self.strip_free[bidx] = mm_m
                        if gi == 0:
                            f_last = self.dve(lambda v, oc=oc, b=bank_i: v.tensor_copy(
                                out=f_sb[:, oc, :], in_=pb[b][:, 0:512]), waits=[mm_m],
                                rd_banks=(bank_i,))
                        else:
                            f_last = self.dve(lambda v, oc=oc, b=bank_i: v.tensor_tensor(
                                out=f_sb[:, oc, :], in0=f_sb[:, oc, :], in1=pb[b][:, 0:512],
                                op=ALU.add), waits=[mm_m], rd_banks=(bank_i,))
                    for eidx, hbuf in emems:
                        hbuf_free[id(hbuf)] = f_last
                    hbuf_free[id(h_shh)] = f_last
                    hbuf_free["we"] = f_last

                self.tap("fsb", lambda: f_sb, [f_last])
                me0c, me1c = emit_ln(f_sb, sqscr_ffn, [f_last])
                mf1 = self.dve(lambda v: v.tensor_tensor(out=lnbuf[:], in0=f_sb, in1=b1rep(),
                                                         op=ALU.mult), waits=[me1c])
                mf2 = self.dve(lambda v: v.tensor_tensor(out=lnbuf[:], in0=lnbuf[:], in1=b0rep(),
                                                         op=ALU.add), waits=[me0c])
                mx2 = self.dve(lambda v: v.tensor_tensor(out=xb[:], in0=xb[:], in1=lnbuf[:],
                                                         op=ALU.add))
                mst = self.dma("gpsimd", lambda t0=t0: self.y_out[:, :, t0:t0 + TT],
                               lambda: xb[:], self.s_out, waits=[mx2])
                st["out_m"] = mst
                st["pe_end"] = down_last
                st["dve_end"] = mx2

            try:
                for ti in range(n_pre):
                    emit_tile(ti, True, self.x_pre, kcp_i, ksp_i)
                mmask = self.dve(lambda v: v.tensor_scalar(
                    out=Sprev.rearrange("p a b -> p (a b)"),
                    in0=Sprev.rearrange("p a b -> p (a b)"), scalar1=maskS_s[:], scalar2=None,
                    op0=ALU.mult), waits=st["sprev_ready"] + [cm["mk"]])
                mmaskb = self.dve(lambda v: v.tensor_copy(out=Sprev_bf[:], in_=Sprev[:]),
                                  waits=[mmask])
                st["sprev_ready"] = [mmaskb]
                st["dve_end"] = mmaskb

                for ti in range(n_tiles):
                    emit_tile(ti, False, self.x_own, kco_i, kso_i)
            except StopEmission:
                pass


            total_out = self.s_out.val
            self.items["gpsimd"].append(((), lambda g: g.wait_ge(
                handles["s_out"], total_out), None, 0))

            def emitter(engname):
                def run(eng):
                    for waits, fn, inc, inc_n in self.items[engname]:
                        for sem, val in waits:
                            eng.wait_ge(handles[sem.name], val)
                        inst = fn(eng)
                        if inc is not None:
                            inst.then_inc(handles[inc.name], inc_n)
                return run
            block.sync(emitter("sync"))
            block.gpsimd(emitter("gpsimd"))
            block.vector(emitter("vector"))
            block.scalar(emitter("scalar"))
            block.tensor(emitter("tensor"))

        return nc


def _fm(a2d, kc):
    k, m = a2d.shape
    assert k == kc * 128
    return np.ascontiguousarray(a2d.reshape(kc, 128, m).transpose(1, 0, 2), dtype=np.float32)


def make_core_inputs(full, core, n_tiles=NT_FULL, n_pre=NT_FULL):
    b, half = core // 2, core % 2
    ntok, npre = n_tiles * TT, max(n_pre, 1) * TT if n_pre else TT
    npre_real = n_pre * TT
    x = np.asarray(full["L0_attn_x"], dtype=np.float32)
    own0 = half * NTOK
    d = {}
    d["x_own"] = _fm(np.ascontiguousarray(x[b, own0:own0 + ntok, :].T), NIC)
    if half == 1 and n_pre:
        x_pre = x[b, NTOK - npre_real:NTOK, :]
        pre0 = NTOK - npre_real
    else:
        x_pre = np.zeros((npre_real if n_pre else TT, D), np.float32)
        pre0 = 0
    xp = np.zeros((D, max(npre_real, TT)), np.float32)
    if n_pre:
        xp[:, :npre_real] = x_pre.T
    d["x_pre"] = _fm(xp, NIC)
    d["wq"] = _fm(np.asarray(full["L0_attn_Wq"]).T, NIC)
    d["wk"] = _fm(np.asarray(full["L0_attn_Wk"]).T, NIC)
    d["wv"] = _fm(np.asarray(full["L0_attn_Wv"]).T, NIC)
    d["wg"] = _fm(np.asarray(full["L0_attn_Wg"]).T, NIC)
    d["wo"] = _fm(np.asarray(full["L0_attn_Wo"]).T, NIC)
    d["wgate"] = _fm(np.asarray(full["L0_ffn_gate_w"]).T, NIC)
    d["wsgw"] = _fm(np.asarray(full["L0_ffn_shared_gw"]).T, NIC)
    up = np.asarray(full["L0_ffn_up_w"])
    dn = np.asarray(full["L0_ffn_down_w"])
    for e in range(NE):
        d[f"wup{e}"] = _fm(up[e].T, NIC)
    sdwT = np.asarray(full["L0_ffn_shared_dw"]).T
    dnT = [dn[e].T for e in range(NE)]
    d["wdnA"] = _fm(np.concatenate([sdwT[:11 * 128], dnT[0], dnT[1]], 0), 33)
    d["wdnB"] = _fm(np.concatenate([sdwT[11 * 128:], dnT[2], dnT[3]], 0), 33)
    d["wdnC"] = _fm(np.concatenate([dnT[4], dnT[5]], 0), 22)
    d["wdnD"] = _fm(np.concatenate([dnT[6], dnT[7]], 0), 22)
    d["bq"] = np.ascontiguousarray(np.asarray(full["L0_attn_bq"], np.float32).reshape(NIC, 128).T)
    d["bo"] = np.ascontiguousarray(np.asarray(full["L0_attn_bo"], np.float32).reshape(NIC, 128).T)
    d["brows"] = np.concatenate(
        [np.asarray(full["L0_attn_bk"]), np.asarray(full["L0_attn_bv"]),
         np.asarray(full["L0_attn_bg"])]).astype(np.float32).reshape(1, 3 * D)
    d["gbias"] = np.ascontiguousarray(
        np.broadcast_to(np.asarray(full["L0_ffn_gate_bias"], np.float32), (128, NE)))
    d["maskS"] = np.full((128, 1), 1.0 if (half == 1 and n_pre) else 0.0, np.float32)
    inv = 1.0 / 10000.0 ** (np.arange(0, DK, 2, dtype=np.float64) / DK)
    pos_own = np.arange(own0, own0 + ntok, dtype=np.float64)
    ang = pos_own[None, :] * inv[:, None]
    cq = np.empty((128, ntok), np.float32)
    sq = np.empty((128, ntok), np.float32)
    cq[0::2] = np.cos(ang); cq[1::2] = np.cos(ang)
    sq[0::2] = -np.sin(ang); sq[1::2] = np.sin(ang)
    d["cosq"], d["sinq"] = np.ascontiguousarray(cq), np.ascontiguousarray(sq)

    def ktab(p0, n):
        pos = np.arange(p0, p0 + n, dtype=np.float64)
        a = pos[:, None] * inv[None, :]
        c = np.cos(a).reshape(n // 128, 128, 64).transpose(1, 0, 2)
        s = np.sin(a).reshape(n // 128, 128, 64).transpose(1, 0, 2)
        return (np.ascontiguousarray(c, np.float32), np.ascontiguousarray(s, np.float32))
    d["kcos_o"], d["ksin_o"] = ktab(own0, ntok)
    d["kcos_p"], d["ksin_p"] = ktab(pre0, max(npre_real, TT))
    d["ident"] = np.eye(128, dtype=np.float32)
    psw = np.zeros((128, 128), np.float32)
    for f in range(64):
        psw[2 * f, 2 * f + 1] = 1.0
        psw[2 * f + 1, 2 * f] = 1.0
    d["pswap"] = psw
    d["onescol"] = np.ones((128, 1), np.float32)
    d["onesrow"] = np.ones((1, 128), np.float32)
    sel8 = np.zeros((8, 8 * 128), np.float32)
    for e in range(NE):
        sel8[e, e * 128:(e + 1) * 128] = 1.0
    d["sel8"] = sel8
    return d


_prog_cache = {}


def get_program(n_tiles=NT_FULL, n_pre=NT_FULL, debug_taps=()):
    key = (n_tiles, n_pre, tuple(debug_taps))
    if key not in _prog_cache:
        _prog_cache[key] = Builder(n_tiles, n_pre, debug_taps).build()
    return _prog_cache[key]


def kernel(**inputs):
    nc = get_program()
    in_maps = [make_core_inputs(inputs, c) for c in range(8)]
    res = run_bass_kernel_spmd(nc, in_maps, core_ids=list(range(8)))
    out = np.empty((B, T, D), np.float32)
    for c in range(8):
        b, half = c // 2, c % 2
        y = res.results[c]["y"]
        y2 = y.transpose(1, 0, 2).reshape(D, NTOK)
        out[b, half * NTOK:(half + 1) * NTOK, :] = y2.T
    return out


# version 03
# speedup vs baseline: 1.1140x; 1.1140x over previous
import numpy as np
import concourse.bass as bass
import concourse.mybir as mybir
from concourse.bass_utils import run_bass_kernel_spmd
from concourse.alu_op_type import AluOpType as ALU
from contextlib import ExitStack

F32 = mybir.dt.float32
BF16 = mybir.dt.bfloat16
AF = mybir.ActivationFunctionType

B, T, D = 4, 8192, 2048
H, DK, CS = 16, 128, 64
NTOK = T // 2
TT = 512
NT_FULL = NTOK // TT
NTB = TT // 128
NCH = TT // CS
NIC = D // 128
HS_SH = 2816
H_E = 1408
NE = 8
EPS_LN = 1e-5
EPS_DEN = 1e-6
SC = 129


class Sem:
    def __init__(self, name):
        self.name, self.val = name, 0

    def inc(self, n=1):
        self.val += n
        return self.val


class StopEmission(Exception):
    pass


class Builder:
    def __init__(self, n_tiles=NT_FULL, n_pre=NT_FULL, debug_taps=(), stop_after_tap=None):
        self.stop_after_tap = stop_after_tap
        self.n_tiles, self.n_pre = n_tiles, n_pre
        self.debug_taps = list(debug_taps)
        self.items = {e: [] for e in ("sync", "gpsimd", "vector", "scalar", "tensor")}
        self.sems = {}
        self.psum_free = [None] * 8
        self.strip_free = [None, None]
        self.strip_idx = 0
        self._sg_guard = None
        self._tapset = set(debug_taps)
        self._tapped = set()

    def _emit(self, eng, fn, waits, inc, inc_n):
        waits = tuple(w for w in waits if w is not None)
        mil = (inc, inc.inc(inc_n)) if inc is not None else None
        self.items[eng].append((waits, fn, inc, inc_n))
        return mil

    def pe(self, fn, waits=(), banks=()):
        w = list(waits) + [self.psum_free[b] for b in banks]
        return self._emit("tensor", fn, w, self.s_pe, 1)

    def dve(self, fn, waits=(), rd_banks=()):
        m = self._emit("vector", fn, waits, self.s_dve, 1)
        for b in rd_banks:
            self.psum_free[b] = m
        return m

    def act(self, fn, waits=(), rd_banks=()):
        m = self._emit("scalar", fn, waits, self.s_act, 1)
        for b in rd_banks:
            self.psum_free[b] = m
        return m

    def pool(self, fn, waits=()):
        return self._emit("gpsimd", fn, waits, self.s_pool, 1)

    def dma(self, queue, out_fn, in_fn, sem, waits=(), **kw):
        def fn(eng):
            return eng.dma_start(out=out_fn(), in_=in_fn(), **kw)
        return self._emit(queue, fn, waits, sem, 16)

    def tap(self, nm, ap_fn, waits):
        if nm not in self._tapset or nm in self._tapped:
            return
        self._tapped.add(nm)
        t = self.nc.dram_tensor("dbg_" + nm, list(ap_fn().shape), ap_fn().dtype,
                                kind="ExternalOutput").ap()
        mil = self.dma("gpsimd", (lambda t=t: t), ap_fn, self.s_out, waits=waits)
        for e in self.items:
            self.items[e].append(((mil,), (lambda eng: eng.nop()), None, 0))
        if self.stop_after_tap == nm:
            raise StopEmission()

    def load_strip(self, nm, kc, col0, cols, extra_waits=()):
        bidx = self.strip_idx % 2
        self.strip_idx += 1
        waits = [self.cache_ready[nm], self.strip_free[bidx]] + list(extra_waits)
        sem = self.s_w0 if bidx == 0 else self.s_w1
        mil = self.dma("sync",
                       (lambda b=bidx, k=kc, c=cols: self.wstrip[:, b, 0:k * c]
                        .rearrange("p (a b) -> p a b", a=k)),
                       (lambda n=nm, k=kc, c0=col0, c=cols: self.w_c[n][:, 0:k, c0:c0 + c]),
                       sem, waits=waits)
        view = self.wstrip[:, bidx, 0:kc * cols].rearrange("p (a b) -> p a b", a=kc)
        return view, mil, bidx

    def build(self):
        n_tiles, n_pre = self.n_tiles, self.n_pre
        ntok, npre = n_tiles * TT, n_pre * TT
        nc = bass.Bass("TRN2", target_bir_lowering=False,
                       detect_race_conditions=False)
        self.nc = nc

        def din(name, shape, dt=F32):
            return nc.dram_tensor(name, shape, dt, kind="ExternalInput").ap()

        self.x_own = din("x_own", [128, NIC, ntok])
        self.x_pre = din("x_pre", [128, NIC, max(npre, TT)])
        w_in = {}
        self.wspec = [("wq", NIC, D), ("wk", NIC, D), ("wv", NIC, D), ("wg", NIC, D),
                      ("wo", NIC, D), ("wsgw", NIC, 2 * HS_SH)] + \
                     [(f"wup{e}", NIC, 2 * H_E) for e in range(NE)] + \
                     [("wdnA", 33, D), ("wdnB", 33, D), ("wdnC", 22, D), ("wdnD", 22, D)]
        for nm, kc, m in self.wspec:
            w_in[nm] = din(nm, [128, kc, m])
        wgate_i = din("wgate", [128, NIC, NE])
        bq_i = din("bq", [128, NIC])
        bo_i = din("bo", [128, NIC])
        brows_i = din("brows", [1, 3 * D])
        gbias_i = din("gbias", [128, NE])
        maskS_i = din("maskS", [128, 1])
        cosq_i = din("cosq", [128, ntok])
        sinq_i = din("sinq", [128, ntok])
        kco_i = din("kcos_o", [128, ntok // 128, 64])
        kso_i = din("ksin_o", [128, ntok // 128, 64])
        kcp_i = din("kcos_p", [128, max(npre, TT) // 128, 64])
        ksp_i = din("ksin_p", [128, max(npre, TT) // 128, 64])
        ident_i = din("ident", [128, 128])
        pswap_i = din("pswap", [128, 128])
        onescol_i = din("onescol", [128, 1])
        onesrow_i = din("onesrow", [1, 128])
        sel8_i = din("sel8", [8, 8 * 128])
        self.y_out = nc.dram_tensor("y", [128, NIC, ntok], F32, kind="ExternalOutput").ap()

        self.w_c = {nm: nc.dram_tensor(nm + "_c", [128, kc, m], BF16, kind="Internal").ap()
                    for nm, kc, m in self.wspec}
        tabspec = [("cosq", [128, ntok]), ("sinq", [128, ntok]),
                   ("kco", [128, ntok // 128, 64]), ("kso", [128, ntok // 128, 64]),
                   ("kcp", [128, max(npre, TT) // 128, 64]),
                   ("ksp", [128, max(npre, TT) // 128, 64])]
        self.tab_c = {nm: nc.dram_tensor(nm + "_c", shp, BF16, kind="Internal").ap()
                      for nm, shp in tabspec}

        es = ExitStack()
        with es:
            sb = lambda name, shape, dt: es.enter_context(nc.sbuf_tensor(name, shape, dt))
            self.xb = xb = sb("xb", [128, NIC, TT], F32)
            self.lnbuf = lnbuf = sb("lnbuf", [128, NIC, TT], BF16)
            self.Sprev = Sprev = sb("Sprev", [128, H, SC], F32)
            self.Sprev_bf = Sprev_bf = sb("Sprev_bf", [128, H, SC], BF16)
            dSbf = sb("dSbf", [128, 2, SC], BF16)
            bq_s = sb("bq_s", [128, NIC], F32)
            bo_s = sb("bo_s", [128, NIC], F32)
            brow_bf = sb("brow_bf", [1, 3 * D], BF16)
            gbias_s = sb("gbias_s", [128, NE], F32)
            maskS_s = sb("maskS_s", [128, 1], F32)
            ident_bf = sb("ident_bf", [128, 128], BF16)
            ident_f = sb("ident_f", [128, 128], F32)
            pswap_bf = sb("pswap_bf", [128, 128], BF16)
            onescol_bf = sb("onescol_bf", [128, 1], BF16)
            onesrow_bf = sb("onesrow_bf", [1, 128], BF16)
            sel8_bf = sb("sel8_bf", [8, 8 * 128], BF16)
            onesrow_f = sb("onesrow_f", [1, 128], F32)
            gate_bf = sb("gate_bf", [128, NIC, NE], BF16)
            b1_sb = sb("b1_sb", [128, TT], F32)
            b0_sb = sb("b0_sb", [128, TT], F32)
            rows4_t = sb("rows4", [1, 3 * TT], F32)
            cosq_s = sb("cosq_s", [128, TT], BF16)
            sinq_s = sb("sinq_s", [128, TT], BF16)
            kcos_s = sb("kcos_s", [128, NTB, 64], BF16)
            ksin_s = sb("ksin_s", [128, NTB, 64], BF16)
            den_s = sb("den_s", [128, 4], F32)
            rd_s = sb("rd_s", [128, 4], F32)
            nonlocd = {}
            aotok = sb("aotok", [128, 2, 128], BF16)
            sfm = sb("sfm", [8, TT], F32)
            stok = sb("stok", [128, NTB, NE], F32)
            r_m1 = sb("r_m1", [128, NTB], F32)
            weT = sb("weT", [8, TT], BF16)
            self.wstrip = sb("wstrip", [128, 2, 8192], BF16)
            ARENA = 42048
            arena = sb("arena", [128, ARENA], BF16)

            def carve(off, shape, dt=BF16):
                n = int(np.prod(shape[1:]))
                if dt == F32:
                    assert off % 2 == 0
                    ap = arena[:, off:off + 2 * n].bitcast(F32)
                    nxt = off + 2 * n
                else:
                    ap = arena[:, off:off + n]
                    nxt = off + n
                if len(shape) > 2:
                    names = [chr(97 + i) for i in range(len(shape) - 1)]
                    r = "p (" + " ".join(names) + ") -> p " + " ".join(names)
                    kw = {names[i]: shape[1 + i] for i in range(len(shape) - 2)}
                    ap = ap.rearrange(r, **kw)
                return ap, nxt

            off = 0
            qbb, off = carve(off, [128, H, TT])
            ktile, off = carve(off, [128, NTB, D])
            k2, off = carve(off, [128, NTB, D])
            v_t, off = carve(off, [128, NTB, H, SC])
            g_t, off = carve(off, [128, NTB, D])
            qhtmp, off = carve(off, [128, 2, TT])
            assert off <= ARENA, f"attn overlay {off}"
            o_f32 = arena[:, 0:2 * NIC * TT].bitcast(F32) \
                .rearrange("p (a b) -> p a b", a=NIC)
            def ao_wr(h, tb):
                return k2[:, tb, h * 128:(h + 1) * 128]
            def ao_rd(ic):
                return k2[:, :, ic * 128:(ic + 1) * 128]
            sqscr_k = arena[:, H * TT:H * TT + NIC * TT].rearrange("p (a b) -> p a b", a=NIC)
            goff = 3 * H * TT + NTB * H * SC
            sqscr_g = arena[:, goff:goff + NIC * TT].rearrange("p (a b) -> p a b", a=NIC)
            off = 0
            h_shh, off = carve(off, [128, 11, TT])
            h_e0, off = carve(off, [128, 11, TT])
            h_e1, off = carve(off, [128, 11, TT])
            sg_e, off = carve(off, [128, 11, TT])
            f_sb, off = carve(off, [128, NIC, TT], F32)
            uw_t, off = carve(off, [128, 2, TT])
            we2, off = carve(off, [128, 2, TT])
            assert off <= ARENA, f"ffn overlay {off}"
            sqscr_ffn = arena[:, 0:NIC * TT].rearrange("p (a b) -> p a b", a=NIC)
            roff = 3 * 11 * TT
            def rcarve(roff, shape):
                n = int(np.prod(shape[1:]))
                ap = arena[:, roff:roff + 2 * n].bitcast(F32)
                if len(shape) == 3:
                    ap = ap.rearrange("p (a b) -> p a b", a=shape[1])
                return ap, roff + 2 * n
            stok2, roff = rcarve(roff, [128, NTB, NE])
            r_t, roff = rcarve(roff, [128, NTB, NE])
            r_w, roff = rcarve(roff, [128, NTB, NE])
            r_m1b, roff = rcarve(roff, [128, NTB])
            r_ws4, roff = rcarve(roff, [128, NTB])
            assert roff <= 4 * 11 * TT

            pb = [es.enter_context(nc.psum_tensor(f"pb{i}", [128, 512], F32)) for i in range(8)]
            handles = {}
            for s in ("s_ld", "s_cast", "s_out", "s_pe", "s_dve", "s_act", "s_pool",
                      "s_w0", "s_w1", "s_x", "s_cd", "s_tab"):
                handles[s] = es.enter_context(nc.semaphore(s))
                setattr(self, s, Sem(s))
            block = es.enter_context(nc.Block())

            cm = {}
            for nm, dst, src in [("bq", lambda: bq_s[:], lambda: bq_i),
                                 ("bo", lambda: bo_s[:], lambda: bo_i),
                                 ("gb", lambda: gbias_s[:], lambda: gbias_i),
                                 ("mk", lambda: maskS_s[:], lambda: maskS_i),
                                 ("idf", lambda: ident_f[:], lambda: ident_i),
                                 ("orf", lambda: onesrow_f[:], lambda: onesrow_i)]:
                cm[nm] = self.dma("sync", dst, src, self.s_ld)
            for nm, dst, src in [("idb", lambda: ident_bf[:], lambda: ident_i),
                                 ("psw", lambda: pswap_bf[:], lambda: pswap_i),
                                 ("ocb", lambda: onescol_bf[:], lambda: onescol_i),
                                 ("orb", lambda: onesrow_bf[:], lambda: onesrow_i),
                                 ("brw", lambda: brow_bf[:], lambda: brows_i),
                                 ("sel8", lambda: sel8_bf[:], lambda: sel8_i),
                                 ("gt", lambda: gate_bf[:], lambda: wgate_i)]:
                cm[nm] = self.dma("gpsimd", dst, src, self.s_cast)
            ld_total = self.s_ld.val
            for k in list(cm.keys()):
                sem0, _v = cm[k]
                if sem0 is self.s_ld:
                    cm[k] = (self.s_ld, ld_total)
            cast_total0 = self.s_cast.val
            self.items["gpsimd"].append((
                ((self.s_cast, cast_total0),),
                lambda g: g.engine_nop(), self.s_cd, 1))
            mcd0 = (self.s_cd, self.s_cd.inc(1) - 0)
            for k in list(cm.keys()):
                sem0, _v = cm[k]
                if sem0 is self.s_cast:
                    cm[k] = mcd0
            self.cm = cm
            mz = self.dve(lambda v: v.memset(Sprev[:], 0.0))
            mzb = self.dve(lambda v: v.memset(Sprev_bf[:], 0.0))

            self.cache_ready = {}
            tab_in = dict(cosq=cosq_i, sinq=sinq_i, kco=kco_i, kso=kso_i, kcp=kcp_i, ksp=ksp_i)
            for nm in ["kcp", "ksp", "cosq", "sinq", "kco", "kso"]:
                self.dma("gpsimd", (lambda n=nm: self.tab_c[n]), (lambda n=nm: tab_in[n]),
                         self.s_cast, max_dma_last_dim=8192)
            self.items["gpsimd"].append((
                ((self.s_cast, self.s_cast.val),),
                lambda g: g.engine_nop(), self.s_cd, 1))
            self.tabs_ready = (self.s_cd, self.s_cd.inc(1))
            order = ["wk", "wv", "wq", "wg", "wo", "wsgw"] + [f"wup{e}" for e in range(2)] + \
                    ["wdnA"] + [f"wup{e}" for e in range(2, NE)] + ["wdnB", "wdnC", "wdnD"]
            for nm in order:
                self.dma("gpsimd", (lambda n=nm: self.w_c[n]), (lambda n=nm: w_in[n]),
                         self.s_cast, max_dma_last_dim=8192)
                self.items["gpsimd"].append((
                    ((self.s_cast, self.s_cast.val),),
                    lambda g: g.engine_nop(), self.s_cd, 1))
                self.cache_ready[nm] = (self.s_cd, self.s_cd.inc(1))

            def b1rep():
                return b1_sb[:, :].unsqueeze(1).broadcast_to((128, NIC, TT))

            def b0rep():
                return b0_sb[:, :].unsqueeze(1).broadcast_to((128, NIC, TT))

            def emit_ln(src_f32, sq_scr, wstats):
                mc = self.dve(lambda v: v.tensor_copy(out=lnbuf[:], in_=src_f32), waits=wstats)
                msq = self.act(lambda s: s.activation(out=sq_scr, in_=src_f32, func=AF.Square),
                               waits=wstats)
                def statmm(bank_i, rhs, wl):
                    def fn(t):
                        for ic in range(NIC):
                            mm = t.matmul(pb[bank_i][0:1, 0:TT], onescol_bf[:],
                                          rhs[:, ic, :], start=(ic == 0), stop=(ic == NIC - 1))
                        return mm
                    return self.pe(fn, waits=wl, banks=(bank_i,))
                m1 = statmm(6, lnbuf, [mc, cm["ocb"]])
                m2 = statmm(7, sq_scr, [msq])
                def smalls(v):
                    v.tensor_scalar(out=rows4_t[0:1, 0 * TT:(0 + 1) * TT], in0=pb[6][0:1, 0:TT], scalar1=1.0 / D,
                                    scalar2=None, op0=ALU.mult)
                    v.tensor_scalar(out=rows4_t[0:1, 1 * TT:(1 + 1) * TT], in0=pb[7][0:1, 0:TT], scalar1=1.0 / D,
                                    scalar2=None, op0=ALU.mult)
                    v.tensor_tensor(out=rows4_t[0:1, 2 * TT:(2 + 1) * TT], in0=rows4_t[0:1, 0 * TT:(0 + 1) * TT], in1=rows4_t[0:1, 0 * TT:(0 + 1) * TT],
                                    op=ALU.mult)
                    v.tensor_tensor(out=rows4_t[0:1, 1 * TT:(1 + 1) * TT], in0=rows4_t[0:1, 1 * TT:(1 + 1) * TT], in1=rows4_t[0:1, 2 * TT:(2 + 1) * TT],
                                    op=ALU.subtract)
                    return v.tensor_scalar(out=rows4_t[0:1, 1 * TT:(1 + 1) * TT], in0=rows4_t[0:1, 1 * TT:(1 + 1) * TT], scalar1=EPS_LN,
                                           scalar2=None, op0=ALU.add)
                ms = self.dve(smalls, waits=[m1, m2], rd_banks=(6, 7))
                ml = self.act(lambda s: s.activation(out=rows4_t[0:1, 1 * TT:(1 + 1) * TT], in_=rows4_t[0:1, 1 * TT:(1 + 1) * TT],
                                                     func=AF.Ln), waits=[ms])
                me = self.act(lambda s: s.activation(out=rows4_t[0:1, 1 * TT:(1 + 1) * TT], in_=rows4_t[0:1, 1 * TT:(1 + 1) * TT],
                                                     func=AF.Exp, scale=-0.5), waits=[ml])
                mn = self.dve(lambda v: (v.tensor_tensor(out=rows4_t[0:1, 2 * TT:(2 + 1) * TT], in0=rows4_t[0:1, 0 * TT:(0 + 1) * TT],
                                                         in1=rows4_t[0:1, 1 * TT:(1 + 1) * TT], op=ALU.mult),
                                         v.tensor_scalar(out=rows4_t[0:1, 2 * TT:(2 + 1) * TT], in0=rows4_t[0:1, 2 * TT:(2 + 1) * TT],
                                                         scalar1=-1.0, scalar2=None,
                                                         op0=ALU.mult))[-1], waits=[me])
                mb1 = self.pe(lambda t: t.matmul(pb[6][:, 0:TT], onesrow_f[:], rows4_t[0:1, 1 * TT:(1 + 1) * TT],
                                                 start=True, stop=True),
                              waits=[mn, cm["orf"]], banks=(6,))
                mb0 = self.pe(lambda t: t.matmul(pb[7][:, 0:TT], onesrow_f[:], rows4_t[0:1, 2 * TT:(2 + 1) * TT],
                                                 start=True, stop=True), waits=[mn], banks=(7,))
                me1 = self.dve(lambda v: v.tensor_copy(out=b1_sb[:], in_=pb[6][:, 0:TT]),
                               waits=[mb1], rd_banks=(6,))
                me0 = self.dve(lambda v: v.tensor_copy(out=b0_sb[:], in_=pb[7][:, 0:TT]),
                               waits=[mb0], rd_banks=(7,))
                return me0, me1

            def emit_fm_proj(wname, out_cols, rhs_ready, evac_fn, extra_mm_waits=()):
                evs = []
                n_oc = out_cols // 128
                oc = 0
                while oc < n_oc:
                    cols = min(512, (n_oc - oc) * 128)
                    wv, wm, bidx = self.load_strip(wname, NIC, oc * 128, cols)
                    last_mm = None
                    for j in range(cols // 128):
                        bank_i = (oc + j) % 2
                        wl = [wm, rhs_ready] + list(extra_mm_waits)
                        def fn(t, wv=wv, j=j, bank_i=bank_i):
                            for ic in range(NIC):
                                mm = t.matmul(pb[bank_i][:, 0:512],
                                              wv[:, ic, j * 128:(j + 1) * 128],
                                              lnbuf[:, ic, :],
                                              start=(ic == 0), stop=(ic == NIC - 1))
                            return mm
                        mm_m = self.pe(fn, waits=wl, banks=(bank_i,))
                        last_mm = mm_m
                        evs.append(evac_fn(oc + j, bank_i, mm_m))
                    self.strip_free[bidx] = last_mm
                    oc += cols // 128
                return evs

            def emit_tokproj(wname, brow_idx, rhs_ready, evac_fn):
                evs = []
                for dc in range(4):
                    wv, wm, bidx = self.load_strip(wname, NIC, dc * 512, 512)
                    last_mm = None
                    for tb in range(NTB):
                        bank_i = (dc * NTB + tb) % 2
                        def fn(t, wv=wv, tb=tb, bank_i=bank_i, dc=dc):
                            t.matmul(pb[bank_i][:, 0:512], onesrow_bf[:],
                                     brow_bf[0:1, brow_idx * D + dc * 512:brow_idx * D + (dc + 1) * 512],
                                     start=True, stop=False)
                            for ic in range(NIC):
                                mm = t.matmul(pb[bank_i][:, 0:512],
                                              lnbuf[:, ic, tb * 128:(tb + 1) * 128],
                                              wv[:, ic, :], start=False, stop=(ic == NIC - 1))
                            return mm
                        mm_m = self.pe(fn, waits=[wm, rhs_ready, cm["orb"], cm["brw"]],
                                       banks=(bank_i,))
                        last_mm = mm_m
                        evs.append(evac_fn(tb, dc, bank_i, mm_m))
                    self.strip_free[bidx] = last_mm
                return evs

            def emit_scan(pre, kv_ready, g_ready, q_ready):
                dSbf4 = self.wstrip[:, 1, 7164:7164 + 4 * SC] \
                    .rearrange("p (a b) -> p a b", a=4)
                aotok4 = self.wstrip[:, 1, 7164 + 4 * SC:7164 + 4 * SC + 512] \
                    .rearrange("p (a b) -> p a b", a=4)
                state_free = [self.psum_free[4 + j] for j in range(4)]
                sprev_last = None
                dve_last = None
                pe_last = None
                prev_mr = [None, None, None, None]
                prev_me = [None, None, None, None]
                last_epi = None
                mtq = {}
                GS = 4 if pre else 2
                for hg in range(H // GS):
                    heads = [GS * hg + j for j in range(GS)]
                    mu = {}
                    ev = {}
                    nm = {}
                    for c in range(NCH):
                        tb, cpar = c // 2, (c % 2) * 64
                        for j, h in enumerate(heads):
                            sbank = pb[4 + j]
                            if c == 0:
                                wl = list(kv_ready) + [state_free[j]]
                            else:
                                wl = [ev[(j, c - 1)]]
                            mu[(j, c)] = self.pe(
                                lambda t, tb=tb, cp=cpar, h=h, c=c, sbank=sbank:
                                t.matmul(sbank[:, 0:SC],
                                         k2[cp:cp + 64, tb, h * 128:(h + 1) * 128],
                                         v_t[cp:cp + 64, tb, h, :],
                                         start=(c == 0), stop=True,
                                         skip_group_check=True), waits=wl)
                            pe_last = mu[(j, c)]
                        for j, h in enumerate(heads):
                            wde = [mu[(j, c)], nm.get((j, c - 1))]
                            ev[(j, c)] = self.dve(
                                lambda v, j=j, sbank=pb[4 + j]: v.tensor_copy(
                                    out=dSbf4[:, j, :], in_=sbank[:, 0:SC]), waits=wde)
                            if pre:
                                continue
                            nbi = 2 + j
                            def numfn(t, h=h, c=c, nbi=nbi, cp=cpar, j=j):
                                t.matmul(pb[nbi][cp:cp + 64, 0:SC],
                                         qbb[:, h, c * CS:(c + 1) * CS], dSbf4[:, j, :],
                                         start=True, stop=False)
                                return t.matmul(pb[nbi][cp:cp + 64, 0:SC],
                                                qbb[:, h, c * CS:(c + 1) * CS],
                                                Sprev_bf[:, h, :], start=False, stop=True)
                            wn = [ev[(j, c)], q_ready]
                            if c % 2 == 0:
                                key = (hg, j, c - 2) if c >= 2 else (hg - 1, j, 6)
                                wn.append(nonlocd.get(key))
                                if hg == 0 and c == 0:
                                    wn.append(self.psum_free[nbi])
                            nm[(j, c)] = self.pe(numfn, waits=wn)
                            pe_last = nm[(j, c)]
                        if pre or c % 2 == 0:
                            continue
                        tb2 = c // 2
                        for j, h in enumerate(heads):
                            nbi = 2 + j
                            md_ = self.dve(lambda v, nbi=nbi, j=j:
                                           v.tensor_scalar(
                                               out=den_s[:, j:j + 1],
                                               in0=pb[nbi][:, 128:129],
                                               scalar1=EPS_DEN, scalar2=None, op0=ALU.add),
                                           waits=[nm[(j, c)], prev_mr[j]])
                            mr_ = self.act(lambda s, j=j: (s.activation(
                                out=rd_s[:, j:j + 1], in_=den_s[:, j:j + 1], func=AF.Ln),
                                s.activation(out=rd_s[:, j:j + 1], in_=rd_s[:, j:j + 1],
                                             func=AF.Exp, scale=-1.0))[-1],
                                waits=[md_, prev_me[j]])
                            prev_mr[j] = mr_
                            def epi(v, nbi=nbi, tb2=tb2, h=h, j=j):
                                return v.scalar_tensor_tensor(
                                    out=aotok4[:, j, :], in0=pb[nbi][:, 0:128],
                                    scalar=rd_s[:, j:j + 1], op0=ALU.mult,
                                    in1=g_t[:, tb2, h * 128:(h + 1) * 128], op1=ALU.mult)
                            wepi = [mr_] + list(g_ready)
                            wepi.append(mtq.get((hg, j, tb2 - 1) if tb2 >= 1
                                                else (hg - 1, j, 3)))
                            me_ = self.dve(epi, waits=wepi)
                            prev_me[j] = me_
                            nonlocd[(hg, j, c - 1)] = me_
                            last_epi = me_
                            tbi = j % 2
                            mt_ = self.pe(lambda t, j=j, tbi=tbi: t.transpose(
                                pb[tbi][:, 0:64].bitcast(BF16), aotok4[:, j, :],
                                ident_bf[:]), waits=[me_, cm["idb"]], banks=(tbi,))
                            mtq[(hg, j, tb2)] = mt_
                            pe_last = mt_
                            mte = self.dve(lambda v, tb2=tb2, h=h, tbi=tbi: v.tensor_copy(
                                out=ao_wr(h, tb2), in_=pb[tbi][:, 0:64].bitcast(BF16)),
                                waits=[mt_], rd_banks=(tbi,))
                            dve_last = mte
                    for j, h in enumerate(heads):
                        wfold = [mu[(j, NCH - 1)]]
                        if not pre:
                            wfold.append(nm[(j, NCH - 1)])
                        msp = self.dve(lambda v, h=h, j=j: v.tensor_tensor(
                            out=Sprev[:, h, :], in0=Sprev[:, h, :], in1=pb[4 + j][:, 0:SC],
                            op=ALU.add), waits=wfold, rd_banks=(4 + j,))
                        mspb = self.dve(lambda v, h=h: v.tensor_copy(
                            out=Sprev_bf[:, h, :], in_=Sprev[:, h, :]), waits=[msp])
                        state_free[j] = mspb
                        sprev_last = mspb
                        if pre:
                            dve_last = mspb
                self.strip_free[0] = pe_last
                self.strip_free[1] = pe_last
                if last_epi is not None:
                    self.psum_free[2] = last_epi
                    self.psum_free[3] = last_epi
                return sprev_last, dve_last

            st = dict(pe_end=None, dve_end=None, out_m=None, rope_free=None,
                      first_tile=True, sprev_ready=[mz, mzb])

            def emit_tile(ti, pre, x_src, kct, kst):
                t0 = ti * TT
                mx = self.dma("sync", lambda: xb[:], lambda: x_src[:, :, t0:t0 + TT],
                              self.s_x, waits=[st["dve_end"], st["out_m"]])
                self.dma("sync", lambda: kcos_s[:],
                         lambda: kct[:, ti * NTB:(ti + 1) * NTB, :], self.s_tab,
                         waits=[st["rope_free"], self.tabs_ready])
                mks = self.dma("sync", lambda: ksin_s[:],
                               lambda: kst[:, ti * NTB:(ti + 1) * NTB, :], self.s_tab)
                mkc = (self.s_tab, self.s_tab.val)
                if not pre:
                    self.dma("sync", lambda: cosq_s[:],
                             lambda: self.tab_c["cosq"][:, t0:t0 + TT], self.s_tab)
                    msq_ = self.dma("sync", lambda: sinq_s[:],
                                    lambda: self.tab_c["sinq"][:, t0:t0 + TT], self.s_tab)
                    mcq = (self.s_tab, self.s_tab.val)
                    mks = mcq

                wstats = [mx, st["pe_end"], st["dve_end"]]
                me0, me1 = emit_ln(xb[:, :, :], sqscr_k, wstats)
                mn1 = self.dve(lambda v: v.tensor_tensor(out=lnbuf[:], in0=xb[:], in1=b1rep(),
                                                         op=ALU.mult), waits=[me1])
                ln_ready = self.dve(lambda v: v.tensor_tensor(out=lnbuf[:], in0=lnbuf[:],
                                                              in1=b0rep(), op=ALU.add),
                                    waits=[me0, mn1])
                if not pre:
                    self.tap("ln0", lambda: lnbuf[:], [ln_ready])
                else:
                    self.tap("pln0", lambda: lnbuf[:], [ln_ready])

                def k_write(tb, dc, bank_i, mm_m):
                    return self.dve(lambda v, tb=tb, dc=dc, b=bank_i: v.tensor_copy(
                        out=ktile[:, tb, dc * 512:(dc + 1) * 512], in_=pb[b][:, 0:512]),
                        waits=[mm_m], rd_banks=(bank_i,))
                k_evs = emit_tokproj("wk", 0, ln_ready, k_write)
                if not pre:
                    self.tap("kraw", lambda: ktile[:], k_evs[-1:])
                kcr = kcos_s[:, :, :].unsqueeze(2).broadcast_to((128, NTB, H, 64))
                ksr = ksin_s[:, :, :].unsqueeze(2).broadcast_to((128, NTB, H, 64))
                def iv(ap, par):
                    return ap.rearrange("p a (h f two) -> p a h f two", h=H, two=2)[:, :, :, :, par]
                kev, kod = iv(ktile, 0), iv(ktile, 1)
                k2ev, k2od = iv(k2, 0), iv(k2, 1)
                scr = iv(g_t, 0)
                mr1 = self.dve(lambda v: v.tensor_tensor(out=k2ev, in0=kev, in1=kcr, op=ALU.mult),
                               waits=k_evs + [mkc, mks])
                mr2 = self.dve(lambda v: v.tensor_tensor(out=k2od, in0=kod, in1=kcr, op=ALU.mult))
                mr3 = self.dve(lambda v: v.tensor_tensor(out=scr, in0=kod, in1=ksr, op=ALU.mult))
                mr4 = self.dve(lambda v: v.tensor_tensor(out=k2ev, in0=k2ev, in1=scr,
                                                         op=ALU.subtract))
                mr5 = self.dve(lambda v: v.tensor_tensor(out=scr, in0=kev, in1=ksr, op=ALU.mult))
                mr6 = self.dve(lambda v: v.tensor_tensor(out=k2od, in0=k2od, in1=scr, op=ALU.add))
                if not pre:
                    self.tap("krope", lambda: k2[:], [mr6])
                mr7 = self.dve(lambda v: v.tensor_scalar(out=ktile, in0=k2, scalar1=0.0,
                                                         scalar2=None, op0=ALU.min), waits=[mr6])
                mr8 = self.act(lambda s: s.activation(out=ktile, in_=ktile, func=AF.Exp),
                               waits=[mr7])
                k_done = self.dve(lambda v: v.scalar_tensor_tensor(
                    out=k2, in0=k2, scalar=0.0, op0=ALU.max, in1=ktile, op1=ALU.add),
                    waits=[mr8])
                rope_free_k = mr5

                mones = self.dve(lambda v: v.memset(v_t[:, :, :, 128:129], 1.0),
                                 waits=[st["pe_end"], st["dve_end"]])
                def v_write(tb, dc, bank_i, mm_m):
                    return self.dve(lambda v, tb=tb, dc=dc, b=bank_i: v.tensor_copy(
                        out=v_t[:, tb, dc * 4:(dc + 1) * 4, 0:128],
                        in_=pb[b][:, 0:512].rearrange("p (a c) -> p a c", a=4)),
                        waits=[mm_m], rd_banks=(bank_i,))
                v_evs = emit_tokproj("wv", 1, ln_ready, v_write)

                g_evs, q_done, rope_free_q = [], None, None
                if not pre:
                    def g_write(tb, dc, bank_i, mm_m):
                        return self.act(lambda s, tb=tb, dc=dc, b=bank_i: s.activation(
                            out=g_t[:, tb, dc * 512:(dc + 1) * 512], in_=pb[b][:, 0:512],
                            func=AF.Sigmoid), waits=[mm_m, mr5], rd_banks=(bank_i,))
                    g_evs = emit_tokproj("wg", 2, ln_ready, g_write)

                    q_evs = []
                    def q_write(oc, bank_i, mm_m):
                        return self.dve(lambda v, oc=oc, b=bank_i: v.tensor_scalar(
                            out=qbb[:, oc, :], in0=pb[b][:, 0:512], scalar1=bq_s[:, oc:oc + 1],
                            scalar2=None, op0=ALU.add), waits=[mm_m, cm["bq"]],
                            rd_banks=(bank_i,))
                    q_evs = emit_fm_proj("wq", D, ln_ready, q_write)
                    q_head_done = []
                    for h in range(H):
                        sbi2 = 2 + h % 2
                        wq1 = [q_evs[h], mcq, msq_]
                        mqc = self.dve(lambda v, h=h: v.tensor_tensor(
                            out=qhtmp[:, 0, :], in0=qbb[:, h, :], in1=cosq_s[:], op=ALU.mult),
                            waits=wq1)
                        msw = self.pe(lambda t, h=h, sbi2=sbi2: t.matmul(
                            pb[sbi2][:, 0:512], pswap_bf[:], qbb[:, h, :], start=True, stop=True),
                            waits=[q_evs[h], cm["psw"]], banks=(sbi2,))
                        msin = self.dve(lambda v, sbi2=sbi2: v.tensor_tensor(
                            out=qhtmp[:, 1, :], in0=pb[sbi2][:, 0:512], in1=sinq_s[:],
                            op=ALU.mult), waits=[msw], rd_banks=(sbi2,))
                        mqr = self.dve(lambda v, h=h: v.tensor_tensor(
                            out=qbb[:, h, :], in0=qhtmp[:, 0, :], in1=qhtmp[:, 1, :], op=ALU.add),
                            waits=[msin, mqc])
                        mmin = self.dve(lambda v, h=h: v.tensor_scalar(
                            out=qhtmp[:, 0, :], in0=qbb[:, h, :], scalar1=0.0, scalar2=None,
                            op0=ALU.min))
                        mex = self.act(lambda s: s.activation(out=qhtmp[:, 1, :],
                                                              in_=qhtmp[:, 0, :], func=AF.Exp),
                                       waits=[mmin])
                        mfin = self.dve(lambda v, h=h: v.scalar_tensor_tensor(
                            out=qbb[:, h, :], in0=qbb[:, h, :], scalar=0.0, op0=ALU.max,
                            in1=qhtmp[:, 1, :], op1=ALU.add), waits=[mex])
                        q_head_done.append(mfin)
                    q_done = q_head_done[-1]
                    rope_free_q = q_head_done[-1]

                if pre:
                    self.tap("pk", lambda: k2[:], [k_done])
                    self.tap("pv", lambda: v_t[:], [v_evs[-1], mones])
                if not pre:
                    self.tap("kf", lambda: k2[:], [k_done])
                    self.tap("vt", lambda: v_t[:], [v_evs[-1], mones])
                    if g_evs:
                        self.tap("gt", lambda: g_t[:], [g_evs[-1]])
                    if q_done:
                        self.tap("qf", lambda: qbb[:], [q_done])
                kv_ready = [k_done, v_evs[-1], mones]
                g_ready = [g_evs[-1]] if g_evs else []
                sprev_last, scan_dve_last = emit_scan(pre, kv_ready, g_ready, q_done)
                st["rope_free"] = rope_free_q or rope_free_k
                if not pre:
                    self.tap("sprev", lambda: Sprev[:], [sprev_last])
                    self.tap("ao", lambda: k2[:], [scan_dve_last])

                if pre:
                    self.tap("pscan", lambda: Sprev[:], [sprev_last])
                    st["pe_end"] = None
                    st["dve_end"] = scan_dve_last
                    st["sprev_ready"] = [sprev_last]
                    return

                def o_write(oc, bank_i, mm_m):
                    return self.dve(lambda v, oc=oc, b=bank_i: v.tensor_scalar(
                        out=o_f32[:, oc, :], in0=pb[b][:, 0:512], scalar1=bo_s[:, oc:oc + 1],
                        scalar2=None, op0=ALU.add), waits=[mm_m, cm["bo"]], rd_banks=(bank_i,))
                def emit_wo():
                    evs = []
                    n_oc = NIC
                    oc = 0
                    while oc < n_oc:
                        wv, wm, bidx = self.load_strip("wo", NIC, oc * 128, 512)
                        last_mm = None
                        for j in range(4):
                            bank_i = (oc + j) % 2
                            def fn(t, wv=wv, j=j, bank_i=bank_i):
                                for ic in range(NIC):
                                    mm = t.matmul(pb[bank_i][:, 0:512],
                                                  wv[:, ic, j * 128:(j + 1) * 128],
                                                  ao_rd(ic),
                                                  start=(ic == 0), stop=(ic == NIC - 1))
                                return mm
                            mm_m = self.pe(fn, waits=[wm, scan_dve_last], banks=(bank_i,))
                            last_mm = mm_m
                            evs.append(o_write(oc + j, bank_i, mm_m))
                        self.strip_free[bidx] = last_mm
                        oc += 4
                    return evs
                o_evs = emit_wo()

                self.tap("o", lambda: o_f32, [o_evs[-1]])
                me0, me1 = emit_ln(o_f32, sqscr_g, [o_evs[-1]])
                mm1 = self.dve(lambda v: v.tensor_tensor(out=lnbuf[:], in0=o_f32, in1=b1rep(),
                                                         op=ALU.mult), waits=[me1])
                mm2 = self.dve(lambda v: v.tensor_tensor(out=lnbuf[:], in0=lnbuf[:], in1=b0rep(),
                                                         op=ALU.add), waits=[me0])
                mx1 = self.dve(lambda v: v.tensor_tensor(out=xb[:], in0=xb[:], in1=lnbuf[:],
                                                         op=ALU.add))

                self.tap("x1", lambda: xb[:], [mx1])
                me0b, me1b = emit_ln(xb[:, :, :], sqscr_g, [mx1])
                mn1b = self.dve(lambda v: v.tensor_tensor(out=lnbuf[:], in0=xb[:], in1=b1rep(),
                                                          op=ALU.mult), waits=[me1b])
                ln1_ready = self.dve(lambda v: v.tensor_tensor(out=lnbuf[:], in0=lnbuf[:],
                                                               in1=b0rep(), op=ALU.add),
                                     waits=[me0b])

                self.tap("ln1", lambda: lnbuf[:], [ln1_ready])
                def gatefn(t):
                    for ic in range(NIC):
                        mm = t.matmul(pb[2][0:8, 0:TT], gate_bf[:, ic, :], lnbuf[:, ic, :],
                                      start=(ic == 0), stop=(ic == NIC - 1))
                    return mm
                mgate = self.pe(gatefn, waits=[ln1_ready, cm["gt"]], banks=(2,))
                msig = self.act(lambda s: s.activation(out=sfm[:], in_=pb[2][0:8, 0:TT],
                                                       func=AF.Sigmoid), waits=[mgate],
                                rd_banks=(2,))
                tr_free_r = None
                mev = None
                for tb in range(NTB):
                    wt = [msig, cm["idf"]]
                    if tr_free_r is not None:
                        wt.append(tr_free_r)
                    mtr = self.pe(lambda t, tb=tb: t.transpose(
                        pb[3][0:128, 0:8], sfm[0:8, tb * 128:(tb + 1) * 128], ident_f[0:8, 0:8]),
                        waits=wt, banks=(3,))
                    mev = self.dve(lambda v, tb=tb: v.tensor_copy(out=stok[:, tb, :],
                                                                  in_=pb[3][0:128, 0:8]),
                                   waits=[mtr], rd_banks=(3,))
                    tr_free_r = mev
                mpc = self.pool(lambda g: g.tensor_copy(out=stok2[:], in_=stok[:]),
                                waits=[mev])
                m1_ = self.dve(lambda v: [v.tensor_tensor(
                    out=r_t[:, tb, :], in0=stok2[:, tb, :], in1=gbias_s[:], op=ALU.add)
                    for tb in range(NTB)][-1], waits=[mpc, cm["gb"]])
                m2_ = self.dve(lambda v: v.tensor_reduce(
                    out=r_m1[:], in_=r_t[:], axis=mybir.AxisListType.X, op=ALU.max),
                    waits=[m1_])
                m2b = self.pool(lambda g: g.tensor_copy(out=r_m1b[:], in_=r_m1[:]),
                                waits=[m2_])
                m3_ = self.dve(lambda v: [v.tensor_scalar(
                    out=r_w[:, tb, :], in0=r_t[:, tb, :], scalar1=r_m1b[:, tb:tb + 1],
                    scalar2=None, op0=ALU.is_ge) for tb in range(NTB)][-1], waits=[m2b])
                m4_ = self.dve(lambda v: [v.scalar_tensor_tensor(
                    out=r_w[:, tb, :], in0=r_w[:, tb, :], scalar=-1e30, op0=ALU.mult,
                    in1=r_t[:, tb, :], op1=ALU.add) for tb in range(NTB)][-1], waits=[m3_])
                m5_ = self.dve(lambda v: v.tensor_reduce(
                    out=r_m1[:], in_=r_w[:], axis=mybir.AxisListType.X, op=ALU.max),
                    waits=[m4_])
                m5b = self.pool(lambda g: g.tensor_copy(out=r_m1b[:], in_=r_m1[:]),
                                waits=[m5_])
                m6_ = self.dve(lambda v: [v.tensor_scalar(
                    out=r_w[:, tb, :], in0=r_t[:, tb, :], scalar1=r_m1b[:, tb:tb + 1],
                    scalar2=None, op0=ALU.is_ge) for tb in range(NTB)][-1], waits=[m5b])
                m7_ = self.dve(lambda v: [v.tensor_tensor(
                    out=r_w[:, tb, :], in0=r_w[:, tb, :], in1=stok2[:, tb, :], op=ALU.mult)
                    for tb in range(NTB)][-1], waits=[m6_])
                m8_ = self.dve(lambda v: v.tensor_reduce(
                    out=r_ws4[:], in_=r_w[:], axis=mybir.AxisListType.X, op=ALU.add),
                    waits=[m7_])
                mrr = self.act(lambda s: (s.activation(out=r_ws4[:], in_=r_ws4[:], func=AF.Ln),
                                          s.activation(out=r_ws4[:], in_=r_ws4[:], func=AF.Exp,
                                                       scale=-1.0))[-1], waits=[m8_])
                route_done = None
                tr_free_w = None
                for tb in range(NTB):
                    mwe_ = self.dve(lambda v, tb=tb: v.tensor_scalar(
                        out=stok[:, tb, :], in0=r_w[:, tb, :], scalar1=r_ws4[:, tb:tb + 1],
                        scalar2=None, op0=ALU.mult), waits=[mrr])
                    mtw = self.pe(lambda t, tb=tb: t.transpose(
                        pb[3][0:8, 0:128], stok[:, tb, :], ident_f[:]),
                        waits=[mwe_] + ([tr_free_w] if tr_free_w else []), banks=(3,))
                    mevw = self.dve(lambda v, tb=tb: v.tensor_copy(
                        out=weT[0:8, tb * 128:(tb + 1) * 128], in_=pb[3][0:8, 0:128]),
                        waits=[mtw], rd_banks=(3,))
                    tr_free_w = mevw
                    route_done = mevw
                self.tap("sfm", lambda: sfm[:], [msig])
                self.tap("we", lambda: stok[:], [route_done])
                uw_guard = [None, None]

                def sh_evac_half(p):
                    def evac(j, oc, bank_i, mm_m):
                        if j < 11:
                            return self.dve(lambda v, j=j, b=bank_i: v.tensor_copy(
                                out=h_shh[:, j, :], in_=pb[b][:, 0:512]), waits=[mm_m],
                                rd_banks=(bank_i,))
                        sl = j % 2
                        ms_ = self.act(lambda s, b=bank_i, sl=sl: s.activation(
                            out=uw_t[:, sl, :], in_=pb[b][:, 0:512], func=AF.Silu),
                            waits=[mm_m, uw_guard[sl]], rd_banks=(bank_i,))
                        mr = self.dve(lambda v, j=j, sl=sl: v.tensor_tensor(
                            out=h_shh[:, j - 11, :], in0=h_shh[:, j - 11, :],
                            in1=uw_t[:, sl, :], op=ALU.mult), waits=[ms_])
                        uw_guard[sl] = mr
                        return mr
                    return evac

                def make_e_evac(eidx, hbuf, we_ready):
                    def evac(j, oc, bank_i, mm_m):
                        if j < 11:
                            m = self.act(lambda s, j=j, b=bank_i: s.activation(
                                out=sg_e[:, j, :], in_=pb[b][:, 0:512], func=AF.Silu),
                                waits=[mm_m, self._sg_guard], rd_banks=(bank_i,))
                            return m
                        sl = j % 2
                        mu = self.dve(lambda v, b=bank_i, e=eidx, sl=sl: v.tensor_tensor(
                            out=uw_t[:, sl, :], in0=pb[b][:, 0:512], in1=we2[:, e % 2, :],
                            op=ALU.mult), waits=[mm_m, we_ready, uw_guard[sl]],
                            rd_banks=(bank_i,))
                        mh = self.dve(lambda v, j=j, hb=hbuf, sl=sl: v.tensor_tensor(
                            out=hb[:, j - 11, :], in0=sg_e[:, j - 11, :], in1=uw_t[:, sl, :],
                            op=ALU.mult), waits=[mu])
                        uw_guard[sl] = mh
                        if j == 21:
                            self._sg_guard = mh
                        return mh
                    return evac

                def emit_cols_proj(wname, oc_list, rhs_ready, evac_fn, extra=()):
                    evs = []
                    pos = 0
                    while pos < len(oc_list):
                        run = [oc_list[pos]]
                        while len(run) < 4 and pos + len(run) < len(oc_list) and \
                                oc_list[pos + len(run)] == run[-1] + 1:
                            run.append(oc_list[pos + len(run)])
                        wv, wm, bidx = self.load_strip(wname, NIC, run[0] * 128, len(run) * 128)
                        last_mm = None
                        for j, oc in enumerate(run):
                            bank_i = (pos + j) % 2
                            wl = [wm, rhs_ready] + list(extra)
                            def fn(t, wv=wv, j=j, bank_i=bank_i):
                                for ic in range(NIC):
                                    mm = t.matmul(pb[bank_i][:, 0:512],
                                                  wv[:, ic, j * 128:(j + 1) * 128],
                                                  lnbuf[:, ic, :],
                                                  start=(ic == 0), stop=(ic == NIC - 1))
                                return mm
                            mm_m = self.pe(fn, waits=wl, banks=(bank_i,))
                            last_mm = mm_m
                            evs.append(evac_fn(pos + j, oc, bank_i, mm_m))
                        self.strip_free[bidx] = last_mm
                        pos += len(run)
                    return evs

                groups = [("wdnA", 0, [(0, h_e0), (1, h_e1)]),
                          ("wdnB", 1, [(2, h_e0), (3, h_e1)]),
                          ("wdnC", None, [(4, h_e0), (5, h_e1)]),
                          ("wdnD", None, [(6, h_e0), (7, h_e1)])]
                f_last = None
                hbuf_free = {}
                for gi, (dnname, shpass, emems) in enumerate(groups):
                    we_ready = None
                    for eidx, _hb in emems:
                        bi = 2 + eidx % 2
                        mwe = self.pe(lambda t, e=eidx, bi=bi: t.matmul(
                            pb[bi][:, 0:TT], sel8_bf[:, e * 128:(e + 1) * 128], weT[0:8, :],
                            start=True, stop=True), waits=[route_done, cm["sel8"]],
                            banks=(bi,))
                        we_ready = self.dve(lambda v, e=eidx, bi=bi: v.tensor_copy(
                            out=we2[:, e % 2, :], in_=pb[bi][:, 0:TT]),
                            waits=[mwe, hbuf_free.get("we")], rd_banks=(bi,))
                    h_ready = []
                    if shpass is not None:
                        ocs = list(range(shpass * 11, shpass * 11 + 11)) + \
                              list(range(22 + shpass * 11, 22 + shpass * 11 + 11))
                        evs = emit_cols_proj("wsgw", ocs, ln1_ready, sh_evac_half(shpass),
                                             extra=[hbuf_free.get(id(h_shh))])
                        h_ready.append(evs[-1])
                    for eidx, hbuf in emems:
                        evs = emit_cols_proj(f"wup{eidx}", list(range(22)), ln1_ready,
                                             make_e_evac(eidx, hbuf, we_ready),
                                             extra=[hbuf_free.get(id(hbuf))])
                        h_ready.append(evs[-1])
                    kc = 33 if shpass is not None else 22
                    rows = ([(h_shh, 0, 11)] if shpass is not None else []) + \
                           [(emems[0][1], kc - 22, 11), (emems[1][1], kc - 11, 11)]
                    down_last = None
                    for oc in range(NIC):
                        wv, wm, bidx = self.load_strip(dnname, kc, oc * 128, 128)
                        bank_i = oc % 2
                        def fn(t, wv=wv, rows=rows, bank_i=bank_i):
                            n_tot = sum(nr for _, _, nr in rows)
                            i = 0
                            for hb, r0, nr in rows:
                                for r in range(nr):
                                    mm = t.matmul(pb[bank_i][:, 0:512], wv[:, r0 + r, :],
                                                  hb[:, r, :], start=(i == 0),
                                                  stop=(i == n_tot - 1))
                                    i += 1
                            return mm
                        mm_m = self.pe(fn, waits=[wm] + h_ready, banks=(bank_i,))
                        down_last = mm_m
                        self.strip_free[bidx] = mm_m
                        if gi == 0:
                            f_last = self.dve(lambda v, oc=oc, b=bank_i: v.tensor_copy(
                                out=f_sb[:, oc, :], in_=pb[b][:, 0:512]), waits=[mm_m],
                                rd_banks=(bank_i,))
                        else:
                            f_last = self.dve(lambda v, oc=oc, b=bank_i: v.tensor_tensor(
                                out=f_sb[:, oc, :], in0=f_sb[:, oc, :], in1=pb[b][:, 0:512],
                                op=ALU.add), waits=[mm_m], rd_banks=(bank_i,))
                    for eidx, hbuf in emems:
                        hbuf_free[id(hbuf)] = f_last
                    hbuf_free[id(h_shh)] = f_last
                    hbuf_free["we"] = f_last

                self.tap("fsb", lambda: f_sb, [f_last])
                me0c, me1c = emit_ln(f_sb, sqscr_ffn, [f_last])
                mf1 = self.dve(lambda v: v.tensor_tensor(out=lnbuf[:], in0=f_sb, in1=b1rep(),
                                                         op=ALU.mult), waits=[me1c])
                mf2 = self.dve(lambda v: v.tensor_tensor(out=lnbuf[:], in0=lnbuf[:], in1=b0rep(),
                                                         op=ALU.add), waits=[me0c])
                mx2 = self.dve(lambda v: v.tensor_tensor(out=xb[:], in0=xb[:], in1=lnbuf[:],
                                                         op=ALU.add))
                mst = self.dma("gpsimd", lambda t0=t0: self.y_out[:, :, t0:t0 + TT],
                               lambda: xb[:], self.s_out, waits=[mx2])
                st["out_m"] = mst
                st["pe_end"] = down_last
                st["dve_end"] = mx2

            try:
                for ti in range(n_pre):
                    emit_tile(ti, True, self.x_pre, self.tab_c["kcp"], self.tab_c["ksp"])
                mmask = self.dve(lambda v: v.tensor_scalar(
                    out=Sprev.rearrange("p a b -> p (a b)"),
                    in0=Sprev.rearrange("p a b -> p (a b)"), scalar1=maskS_s[:], scalar2=None,
                    op0=ALU.mult), waits=st["sprev_ready"] + [cm["mk"]])
                mmaskb = self.dve(lambda v: v.tensor_copy(out=Sprev_bf[:], in_=Sprev[:]),
                                  waits=[mmask])
                st["sprev_ready"] = [mmaskb]
                st["dve_end"] = mmaskb

                for ti in range(n_tiles):
                    emit_tile(ti, False, self.x_own, self.tab_c["kco"], self.tab_c["kso"])
            except StopEmission:
                pass


            total_out = self.s_out.val
            self.items["gpsimd"].append(((), lambda g: g.wait_ge(
                handles["s_out"], total_out), None, 0))

            def emitter(engname):
                def run(eng):
                    for waits, fn, inc, inc_n in self.items[engname]:
                        for sem, val in waits:
                            eng.wait_ge(handles[sem.name], val)
                        inst = fn(eng)
                        if inc is not None:
                            inst.then_inc(handles[inc.name], inc_n)
                return run
            block.sync(emitter("sync"))
            block.gpsimd(emitter("gpsimd"))
            block.vector(emitter("vector"))
            block.scalar(emitter("scalar"))
            block.tensor(emitter("tensor"))

        return nc


def _fm(a2d, kc):
    k, m = a2d.shape
    assert k == kc * 128
    return np.ascontiguousarray(a2d.reshape(kc, 128, m).transpose(1, 0, 2), dtype=np.float32)


def make_core_inputs(full, core, n_tiles=NT_FULL, n_pre=NT_FULL):
    b, half = core // 2, core % 2
    ntok, npre = n_tiles * TT, max(n_pre, 1) * TT if n_pre else TT
    npre_real = n_pre * TT
    x = np.asarray(full["L0_attn_x"], dtype=np.float32)
    own0 = half * NTOK
    d = {}
    d["x_own"] = _fm(np.ascontiguousarray(x[b, own0:own0 + ntok, :].T), NIC)
    if half == 1 and n_pre:
        x_pre = x[b, NTOK - npre_real:NTOK, :]
        pre0 = NTOK - npre_real
    else:
        x_pre = np.zeros((npre_real if n_pre else TT, D), np.float32)
        pre0 = 0
    xp = np.zeros((D, max(npre_real, TT)), np.float32)
    if n_pre:
        xp[:, :npre_real] = x_pre.T
    d["x_pre"] = _fm(xp, NIC)
    d["wq"] = _fm(np.asarray(full["L0_attn_Wq"]).T, NIC)
    d["wk"] = _fm(np.asarray(full["L0_attn_Wk"]).T, NIC)
    d["wv"] = _fm(np.asarray(full["L0_attn_Wv"]).T, NIC)
    d["wg"] = _fm(np.asarray(full["L0_attn_Wg"]).T, NIC)
    d["wo"] = _fm(np.asarray(full["L0_attn_Wo"]).T, NIC)
    d["wgate"] = _fm(np.asarray(full["L0_ffn_gate_w"]).T, NIC)
    d["wsgw"] = _fm(np.asarray(full["L0_ffn_shared_gw"]).T, NIC)
    up = np.asarray(full["L0_ffn_up_w"])
    dn = np.asarray(full["L0_ffn_down_w"])
    for e in range(NE):
        d[f"wup{e}"] = _fm(up[e].T, NIC)
    sdwT = np.asarray(full["L0_ffn_shared_dw"]).T
    dnT = [dn[e].T for e in range(NE)]
    d["wdnA"] = _fm(np.concatenate([sdwT[:11 * 128], dnT[0], dnT[1]], 0), 33)
    d["wdnB"] = _fm(np.concatenate([sdwT[11 * 128:], dnT[2], dnT[3]], 0), 33)
    d["wdnC"] = _fm(np.concatenate([dnT[4], dnT[5]], 0), 22)
    d["wdnD"] = _fm(np.concatenate([dnT[6], dnT[7]], 0), 22)
    d["bq"] = np.ascontiguousarray(np.asarray(full["L0_attn_bq"], np.float32).reshape(NIC, 128).T)
    d["bo"] = np.ascontiguousarray(np.asarray(full["L0_attn_bo"], np.float32).reshape(NIC, 128).T)
    d["brows"] = np.concatenate(
        [np.asarray(full["L0_attn_bk"]), np.asarray(full["L0_attn_bv"]),
         np.asarray(full["L0_attn_bg"])]).astype(np.float32).reshape(1, 3 * D)
    d["gbias"] = np.ascontiguousarray(
        np.broadcast_to(np.asarray(full["L0_ffn_gate_bias"], np.float32), (128, NE)))
    d["maskS"] = np.full((128, 1), 1.0 if (half == 1 and n_pre) else 0.0, np.float32)
    inv = 1.0 / 10000.0 ** (np.arange(0, DK, 2, dtype=np.float64) / DK)
    pos_own = np.arange(own0, own0 + ntok, dtype=np.float64)
    ang = pos_own[None, :] * inv[:, None]
    cq = np.empty((128, ntok), np.float32)
    sq = np.empty((128, ntok), np.float32)
    cq[0::2] = np.cos(ang); cq[1::2] = np.cos(ang)
    sq[0::2] = -np.sin(ang); sq[1::2] = np.sin(ang)
    d["cosq"], d["sinq"] = np.ascontiguousarray(cq), np.ascontiguousarray(sq)

    def ktab(p0, n):
        pos = np.arange(p0, p0 + n, dtype=np.float64)
        a = pos[:, None] * inv[None, :]
        c = np.cos(a).reshape(n // 128, 128, 64).transpose(1, 0, 2)
        s = np.sin(a).reshape(n // 128, 128, 64).transpose(1, 0, 2)
        return (np.ascontiguousarray(c, np.float32), np.ascontiguousarray(s, np.float32))
    d["kcos_o"], d["ksin_o"] = ktab(own0, ntok)
    d["kcos_p"], d["ksin_p"] = ktab(pre0, max(npre_real, TT))
    d["ident"] = np.eye(128, dtype=np.float32)
    psw = np.zeros((128, 128), np.float32)
    for f in range(64):
        psw[2 * f, 2 * f + 1] = 1.0
        psw[2 * f + 1, 2 * f] = 1.0
    d["pswap"] = psw
    d["onescol"] = np.ones((128, 1), np.float32)
    d["onesrow"] = np.ones((1, 128), np.float32)
    sel8 = np.zeros((8, 8 * 128), np.float32)
    for e in range(NE):
        sel8[e, e * 128:(e + 1) * 128] = 1.0
    d["sel8"] = sel8
    return d


_prog_cache = {}


def get_program(n_tiles=NT_FULL, n_pre=NT_FULL, debug_taps=()):
    key = (n_tiles, n_pre, tuple(debug_taps))
    if key not in _prog_cache:
        _prog_cache[key] = Builder(n_tiles, n_pre, debug_taps).build()
    return _prog_cache[key]


def kernel(**inputs):
    nc = get_program()
    in_maps = [make_core_inputs(inputs, c) for c in range(8)]
    res = run_bass_kernel_spmd(nc, in_maps, core_ids=list(range(8)))
    out = np.empty((B, T, D), np.float32)
    for c in range(8):
        b, half = c // 2, c % 2
        y = res.results[c]["y"]
        y2 = y.transpose(1, 0, 2).reshape(D, NTOK)
        out[b, half * NTOK:(half + 1) * NTOK, :] = y2.T
    return out


# version 05
# speedup vs baseline: 1.1442x; 1.0271x over previous
import numpy as np
import concourse.bass as bass
import concourse.mybir as mybir
from concourse.bass_utils import run_bass_kernel_spmd
from concourse.alu_op_type import AluOpType as ALU
from contextlib import ExitStack

F32 = mybir.dt.float32
BF16 = mybir.dt.bfloat16
AF = mybir.ActivationFunctionType

B, T, D = 4, 8192, 2048
H, DK, CS = 16, 128, 64
NTOK = T // 2
TT = 512
NT_FULL = NTOK // TT
NTB = TT // 128
NCH = TT // CS
NIC = D // 128
HS_SH = 2816
H_E = 1408
NE = 8
EPS_LN = 1e-5
EPS_DEN = 1e-6
SC = 129


class Sem:
    def __init__(self, name):
        self.name, self.val = name, 0

    def inc(self, n=1):
        self.val += n
        return self.val


class StopEmission(Exception):
    pass


class Builder:
    def __init__(self, n_tiles=NT_FULL, n_pre=NT_FULL, debug_taps=(), stop_after_tap=None):
        self.stop_after_tap = stop_after_tap
        self.n_tiles, self.n_pre = n_tiles, n_pre
        self.debug_taps = list(debug_taps)
        self.items = {e: [] for e in ("sync", "gpsimd", "vector", "scalar", "tensor")}
        self.sems = {}
        self.psum_free = [None] * 8
        self.strip_free = [None, None]
        self.strip_idx = 0
        self._sg_guard = None
        self._b1_free = None
        self._tapset = set(debug_taps)
        self._tapped = set()

    def _emit(self, eng, fn, waits, inc, inc_n):
        waits = tuple(w for w in waits if w is not None)
        mil = (inc, inc.inc(inc_n)) if inc is not None else None
        self.items[eng].append((waits, fn, inc, inc_n))
        return mil

    def pe(self, fn, waits=(), banks=()):
        w = list(waits) + [self.psum_free[b] for b in banks]
        return self._emit("tensor", fn, w, self.s_pe, 1)

    def dve(self, fn, waits=(), rd_banks=()):
        m = self._emit("vector", fn, waits, self.s_dve, 1)
        for b in rd_banks:
            self.psum_free[b] = m
        return m

    def act(self, fn, waits=(), rd_banks=()):
        m = self._emit("scalar", fn, waits, self.s_act, 1)
        for b in rd_banks:
            self.psum_free[b] = m
        return m

    def pool(self, fn, waits=()):
        return self._emit("gpsimd", fn, waits, self.s_pool, 1)

    def dma(self, queue, out_fn, in_fn, sem, waits=(), **kw):
        def fn(eng):
            return eng.dma_start(out=out_fn(), in_=in_fn(), **kw)
        return self._emit(queue, fn, waits, sem, 16)

    def tap(self, nm, ap_fn, waits):
        if nm not in self._tapset or nm in self._tapped:
            return
        self._tapped.add(nm)
        t = self.nc.dram_tensor("dbg_" + nm, list(ap_fn().shape), ap_fn().dtype,
                                kind="ExternalOutput").ap()
        mil = self.dma("gpsimd", (lambda t=t: t), ap_fn, self.s_out, waits=waits)
        for e in self.items:
            self.items[e].append(((mil,), (lambda eng: eng.nop()), None, 0))
        if self.stop_after_tap == nm:
            raise StopEmission()

    def load_strip(self, nm, kc, col0, cols, extra_waits=()):
        bidx = self.strip_idx % 2
        self.strip_idx += 1
        waits = [self.cache_ready[nm], self.strip_free[bidx]] + list(extra_waits)
        sem = self.s_w0 if bidx == 0 else self.s_w1
        mil = self.dma("sync",
                       (lambda b=bidx, k=kc, c=cols: self.wstrip[:, b, 0:k * c]
                        .rearrange("p (a b) -> p a b", a=k)),
                       (lambda n=nm, k=kc, c0=col0, c=cols: self.w_c[n][:, 0:k, c0:c0 + c]),
                       sem, waits=waits)
        view = self.wstrip[:, bidx, 0:kc * cols].rearrange("p (a b) -> p a b", a=kc)
        return view, mil, bidx

    def build(self):
        n_tiles, n_pre = self.n_tiles, self.n_pre
        ntok, npre = n_tiles * TT, n_pre * TT
        nc = bass.Bass("TRN2", target_bir_lowering=False,
                       detect_race_conditions=False)
        self.nc = nc

        def din(name, shape, dt=F32):
            return nc.dram_tensor(name, shape, dt, kind="ExternalInput").ap()

        self.x_own = din("x_own", [128, NIC, ntok])
        self.x_pre = din("x_pre", [128, NIC, max(npre, TT)])
        w_in = {}
        self.wspec = [("wq", NIC, D), ("wk", NIC, D), ("wv", NIC, D), ("wg", NIC, D),
                      ("wo", NIC, D), ("wsgw", NIC, 2 * HS_SH)] + \
                     [(f"wup{e}", NIC, 2 * H_E) for e in range(NE)] + \
                     [("wdnA", 33, D), ("wdnB", 33, D), ("wdnC", 22, D), ("wdnD", 22, D)]
        for nm, kc, m in self.wspec:
            w_in[nm] = din(nm, [128, kc, m])
        wgate_i = din("wgate", [128, NIC, NE])
        bq_i = din("bq", [128, NIC])
        bo_i = din("bo", [128, NIC])
        brows_i = din("brows", [1, 3 * D])
        gbias_i = din("gbias", [128, NE])
        maskS_i = din("maskS", [128, 1])
        cosq_i = din("cosq", [128, ntok])
        sinq_i = din("sinq", [128, ntok])
        kco_i = din("kcos_o", [128, ntok // 128, 64])
        kso_i = din("ksin_o", [128, ntok // 128, 64])
        kcp_i = din("kcos_p", [128, max(npre, TT) // 128, 64])
        ksp_i = din("ksin_p", [128, max(npre, TT) // 128, 64])
        ident_i = din("ident", [128, 128])
        pswap_i = din("pswap", [128, 128])
        onescol_i = din("onescol", [128, 1])
        onesrow_i = din("onesrow", [1, 128])
        sel8_i = din("sel8", [8, 8 * 128])
        self.y_out = nc.dram_tensor("y", [128, NIC, ntok], F32, kind="ExternalOutput").ap()

        self.w_c = {nm: nc.dram_tensor(nm + "_c", [128, kc, m], BF16, kind="Internal").ap()
                    for nm, kc, m in self.wspec}
        tabspec = [("cosq", [128, ntok]), ("sinq", [128, ntok]),
                   ("kco", [128, ntok // 128, 64]), ("kso", [128, ntok // 128, 64]),
                   ("kcp", [128, max(npre, TT) // 128, 64]),
                   ("ksp", [128, max(npre, TT) // 128, 64])]
        self.tab_c = {nm: nc.dram_tensor(nm + "_c", shp, BF16, kind="Internal").ap()
                      for nm, shp in tabspec}

        es = ExitStack()
        with es:
            sb = lambda name, shape, dt: es.enter_context(nc.sbuf_tensor(name, shape, dt))
            self.xb = xb = sb("xb", [128, NIC, TT], F32)
            self.lnbuf = lnbuf = sb("lnbuf", [128, NIC, TT], BF16)
            self.Sprev = Sprev = sb("Sprev", [128, H, SC], F32)
            self.Sprev_bf = Sprev_bf = sb("Sprev_bf", [128, H, SC], BF16)
            dSbf = sb("dSbf", [128, 2, SC], BF16)
            bq_s = sb("bq_s", [128, NIC], F32)
            bo_s = sb("bo_s", [128, NIC], F32)
            brow_bf = sb("brow_bf", [1, 3 * D], BF16)
            gbias_s = sb("gbias_s", [128, NE], F32)
            maskS_s = sb("maskS_s", [128, 1], F32)
            ident_bf = sb("ident_bf", [128, 128], BF16)
            ident_f = sb("ident_f", [128, 128], F32)
            pswap_bf = sb("pswap_bf", [128, 128], BF16)
            onescol_bf = sb("onescol_bf", [128, 1], BF16)
            onesrow_bf = sb("onesrow_bf", [1, 128], BF16)
            sel8_bf = sb("sel8_bf", [8, 8 * 128], BF16)
            onesrow_f = sb("onesrow_f", [1, 128], F32)
            gate_bf = sb("gate_bf", [128, NIC, NE], BF16)
            b1_sb = sb("b1_sb", [128, TT], F32)
            b0_sb = sb("b0_sb", [128, TT], F32)
            rows4_t = sb("rows4", [1, 3 * TT], F32)
            cosq_s = sb("cosq_s", [128, TT], BF16)
            sinq_s = sb("sinq_s", [128, TT], BF16)
            kcos_s = sb("kcos_s", [128, NTB, 64], BF16)
            ksin_s = sb("ksin_s", [128, NTB, 64], BF16)
            den_s = sb("den_s", [128, 4], F32)
            rd_s = sb("rd_s", [128, 4], F32)
            nonlocd = {}
            aotok = sb("aotok", [128, 2, 128], BF16)
            sfm = sb("sfm", [8, TT], F32)
            stok = sb("stok", [128, NTB, NE], F32)
            r_m1 = sb("r_m1", [128, NTB], F32)
            weT = sb("weT", [8, TT], BF16)
            self.wstrip = sb("wstrip", [128, 2, 8192], BF16)
            ARENA = 42048
            arena = sb("arena", [128, ARENA], BF16)

            def carve(off, shape, dt=BF16):
                n = int(np.prod(shape[1:]))
                if dt == F32:
                    assert off % 2 == 0
                    ap = arena[:, off:off + 2 * n].bitcast(F32)
                    nxt = off + 2 * n
                else:
                    ap = arena[:, off:off + n]
                    nxt = off + n
                if len(shape) > 2:
                    names = [chr(97 + i) for i in range(len(shape) - 1)]
                    r = "p (" + " ".join(names) + ") -> p " + " ".join(names)
                    kw = {names[i]: shape[1 + i] for i in range(len(shape) - 2)}
                    ap = ap.rearrange(r, **kw)
                return ap, nxt

            off = 0
            qbb, off = carve(off, [128, H, TT])
            ktile, off = carve(off, [128, NTB, D])
            k2, off = carve(off, [128, NTB, D])
            v_t, off = carve(off, [128, NTB, H, SC])
            g_t, off = carve(off, [128, NTB, D])
            qhtmp, off = carve(off, [128, 2, TT])
            assert off <= ARENA, f"attn overlay {off}"
            o_f32 = arena[:, 0:2 * NIC * TT].bitcast(F32) \
                .rearrange("p (a b) -> p a b", a=NIC)
            def ao_wr(h, tb):
                return k2[:, tb, h * 128:(h + 1) * 128]
            def ao_rd(ic):
                return k2[:, :, ic * 128:(ic + 1) * 128]
            sqscr_k = arena[:, H * TT:H * TT + NIC * TT].rearrange("p (a b) -> p a b", a=NIC)
            goff = 3 * H * TT + NTB * H * SC
            sqscr_g = arena[:, goff:goff + NIC * TT].rearrange("p (a b) -> p a b", a=NIC)
            off = 0
            h_shh, off = carve(off, [128, 11, TT])
            h_e0, off = carve(off, [128, 11, TT])
            h_e1, off = carve(off, [128, 11, TT])
            sg_e, off = carve(off, [128, 11, TT])
            f_sb, off = carve(off, [128, NIC, TT], F32)
            uw_t, off = carve(off, [128, 2, TT])
            we2, off = carve(off, [128, 2, TT])
            assert off <= ARENA, f"ffn overlay {off}"
            sqscr_ffn = arena[:, 0:NIC * TT].rearrange("p (a b) -> p a b", a=NIC)
            ao_tok_t = arena[:, H * TT:H * TT + NTB * D] \
                .rearrange("p (a b) -> p a b", a=NTB)
            roff = 3 * 11 * TT
            def rcarve(roff, shape):
                n = int(np.prod(shape[1:]))
                ap = arena[:, roff:roff + 2 * n].bitcast(F32)
                if len(shape) == 3:
                    ap = ap.rearrange("p (a b) -> p a b", a=shape[1])
                return ap, roff + 2 * n
            stok2, roff = rcarve(roff, [128, NTB, NE])
            r_t, roff = rcarve(roff, [128, NTB, NE])
            r_w, roff = rcarve(roff, [128, NTB, NE])
            r_m1b, roff = rcarve(roff, [128, NTB])
            r_ws4, roff = rcarve(roff, [128, NTB])
            assert roff <= 4 * 11 * TT

            pb = [es.enter_context(nc.psum_tensor(f"pb{i}", [128, 512], F32)) for i in range(8)]
            handles = {}
            for s in ("s_ld", "s_cast", "s_out", "s_pe", "s_dve", "s_act", "s_pool",
                      "s_w0", "s_w1", "s_x", "s_cd", "s_tab",
                      "s_x0", "s_x1", "s_x2", "s_x3",
                      "s_o0", "s_o1", "s_o2", "s_o3"):
                handles[s] = es.enter_context(nc.semaphore(s))
                setattr(self, s, Sem(s))
            block = es.enter_context(nc.Block())

            cm = {}
            for nm, dst, src in [("bq", lambda: bq_s[:], lambda: bq_i),
                                 ("bo", lambda: bo_s[:], lambda: bo_i),
                                 ("gb", lambda: gbias_s[:], lambda: gbias_i),
                                 ("mk", lambda: maskS_s[:], lambda: maskS_i),
                                 ("idf", lambda: ident_f[:], lambda: ident_i),
                                 ("orf", lambda: onesrow_f[:], lambda: onesrow_i)]:
                cm[nm] = self.dma("sync", dst, src, self.s_ld)
            for nm, dst, src in [("idb", lambda: ident_bf[:], lambda: ident_i),
                                 ("psw", lambda: pswap_bf[:], lambda: pswap_i),
                                 ("ocb", lambda: onescol_bf[:], lambda: onescol_i),
                                 ("orb", lambda: onesrow_bf[:], lambda: onesrow_i),
                                 ("brw", lambda: brow_bf[:], lambda: brows_i),
                                 ("sel8", lambda: sel8_bf[:], lambda: sel8_i),
                                 ("gt", lambda: gate_bf[:], lambda: wgate_i)]:
                cm[nm] = self.dma("gpsimd", dst, src, self.s_cast)
            ld_total = self.s_ld.val
            for k in list(cm.keys()):
                sem0, _v = cm[k]
                if sem0 is self.s_ld:
                    cm[k] = (self.s_ld, ld_total)
            cast_total0 = self.s_cast.val
            self.items["gpsimd"].append((
                ((self.s_cast, cast_total0),),
                lambda g: g.engine_nop(), self.s_cd, 1))
            mcd0 = (self.s_cd, self.s_cd.inc(1) - 0)
            for k in list(cm.keys()):
                sem0, _v = cm[k]
                if sem0 is self.s_cast:
                    cm[k] = mcd0
            self.cm = cm
            mz = self.dve(lambda v: v.memset(Sprev[:], 0.0))
            mzb = self.dve(lambda v: v.memset(Sprev_bf[:], 0.0))

            self.cache_ready = {}
            tab_in = dict(cosq=cosq_i, sinq=sinq_i, kco=kco_i, kso=kso_i, kcp=kcp_i, ksp=ksp_i)
            for nm in ["kcp", "ksp", "cosq", "sinq", "kco", "kso"]:
                self.dma("gpsimd", (lambda n=nm: self.tab_c[n]), (lambda n=nm: tab_in[n]),
                         self.s_cast, max_dma_last_dim=8192)
            self.items["gpsimd"].append((
                ((self.s_cast, self.s_cast.val),),
                lambda g: g.engine_nop(), self.s_cd, 1))
            self.tabs_ready = (self.s_cd, self.s_cd.inc(1))
            order = ["wk", "wv", "wq", "wg", "wo", "wsgw"] + [f"wup{e}" for e in range(2)] + \
                    ["wdnA"] + [f"wup{e}" for e in range(2, NE)] + ["wdnB", "wdnC", "wdnD"]
            for nm in order:
                self.dma("gpsimd", (lambda n=nm: self.w_c[n]), (lambda n=nm: w_in[n]),
                         self.s_cast, max_dma_last_dim=8192)
                self.items["gpsimd"].append((
                    ((self.s_cast, self.s_cast.val),),
                    lambda g: g.engine_nop(), self.s_cd, 1))
                self.cache_ready[nm] = (self.s_cd, self.s_cd.inc(1))

            def b1rep():
                return b1_sb[:, :].unsqueeze(1).broadcast_to((128, NIC, TT))

            def b0rep():
                return b0_sb[:, :].unsqueeze(1).broadcast_to((128, NIC, TT))

            def emit_ln(src_f32, sq_scr, wstats, quarters=None, sq_gate=None):
                qw = quarters if quarters is not None else [None] * 4
                mcq_, msqq_ = [], []
                for qi in range(4):
                    sl = slice(4 * qi, 4 * qi + 4)
                    w = list(wstats) + [qw[qi]]
                    mcq_.append(self.dve(lambda v, sl=sl: v.tensor_copy(
                        out=lnbuf[:, sl, :], in_=src_f32[:, sl, :]), waits=w))
                    msqq_.append(self.act(lambda s, sl=sl: s.activation(
                        out=sq_scr[:, sl, :], in_=src_f32[:, sl, :], func=AF.Square),
                        waits=w + [sq_gate]))
                def statmm(bank_i, rhs, wl, qi):
                    def fn(t):
                        for ic in range(4 * qi, 4 * qi + 4):
                            mm = t.matmul(pb[bank_i][0:1, 0:TT], onescol_bf[:],
                                          rhs[:, ic, :], start=(ic == 0),
                                          stop=(ic == NIC - 1), skip_group_check=True)
                        return mm
                    return self.pe(fn, waits=wl, banks=(bank_i,) if qi == 0 else ())
                m1 = m2 = None
                for qi in range(4):
                    m1 = statmm(6, lnbuf, [mcq_[qi], cm["ocb"]], qi)
                    m2 = statmm(7, sq_scr, [msqq_[qi]], qi)
                def smalls(v):
                    v.tensor_scalar(out=rows4_t[0:1, 0 * TT:(0 + 1) * TT], in0=pb[6][0:1, 0:TT], scalar1=1.0 / D,
                                    scalar2=None, op0=ALU.mult)
                    v.tensor_scalar(out=rows4_t[0:1, 1 * TT:(1 + 1) * TT], in0=pb[7][0:1, 0:TT], scalar1=1.0 / D,
                                    scalar2=None, op0=ALU.mult)
                    v.tensor_tensor(out=rows4_t[0:1, 2 * TT:(2 + 1) * TT], in0=rows4_t[0:1, 0 * TT:(0 + 1) * TT], in1=rows4_t[0:1, 0 * TT:(0 + 1) * TT],
                                    op=ALU.mult)
                    v.tensor_tensor(out=rows4_t[0:1, 1 * TT:(1 + 1) * TT], in0=rows4_t[0:1, 1 * TT:(1 + 1) * TT], in1=rows4_t[0:1, 2 * TT:(2 + 1) * TT],
                                    op=ALU.subtract)
                    return v.tensor_scalar(out=rows4_t[0:1, 1 * TT:(1 + 1) * TT], in0=rows4_t[0:1, 1 * TT:(1 + 1) * TT], scalar1=EPS_LN,
                                           scalar2=None, op0=ALU.add)
                ms = self.dve(smalls, waits=[m1, m2], rd_banks=(6, 7))
                ml = self.act(lambda s: s.activation(out=rows4_t[0:1, 1 * TT:(1 + 1) * TT], in_=rows4_t[0:1, 1 * TT:(1 + 1) * TT],
                                                     func=AF.Ln), waits=[ms])
                me = self.act(lambda s: s.activation(out=rows4_t[0:1, 1 * TT:(1 + 1) * TT], in_=rows4_t[0:1, 1 * TT:(1 + 1) * TT],
                                                     func=AF.Exp, scale=-0.5), waits=[ml])
                mn = self.dve(lambda v: (v.tensor_tensor(out=rows4_t[0:1, 2 * TT:(2 + 1) * TT], in0=rows4_t[0:1, 0 * TT:(0 + 1) * TT],
                                                         in1=rows4_t[0:1, 1 * TT:(1 + 1) * TT], op=ALU.mult),
                                         v.tensor_scalar(out=rows4_t[0:1, 2 * TT:(2 + 1) * TT], in0=rows4_t[0:1, 2 * TT:(2 + 1) * TT],
                                                         scalar1=-1.0, scalar2=None,
                                                         op0=ALU.mult))[-1], waits=[me])
                mb1 = self.pe(lambda t: t.matmul(pb[6][:, 0:TT], onesrow_f[:], rows4_t[0:1, 1 * TT:(1 + 1) * TT],
                                                 start=True, stop=True),
                              waits=[mn, cm["orf"]], banks=(6,))
                mb0 = self.pe(lambda t: t.matmul(pb[7][:, 0:TT], onesrow_f[:], rows4_t[0:1, 2 * TT:(2 + 1) * TT],
                                                 start=True, stop=True), waits=[mn], banks=(7,))
                me1 = self.dve(lambda v: v.tensor_copy(out=b1_sb[:], in_=pb[6][:, 0:TT]),
                               waits=[mb1, self._b1_free], rd_banks=(6,))
                me0 = self.dve(lambda v: v.tensor_copy(out=b0_sb[:], in_=pb[7][:, 0:TT]),
                               waits=[mb0, self._b1_free], rd_banks=(7,))
                return me0, me1

            def b1h(lo):
                return b1_sb[:, :].unsqueeze(1).broadcast_to((128, 8, TT))

            def emit_norm(dst, src_f32, me0, me1, resid=None, out_dt_note=None):
                lo, hi = slice(0, 8), slice(8, 16)
                mA = self.dve(lambda v: v.tensor_tensor(
                    out=dst[:, lo, :], in0=src_f32[:, lo, :], in1=b1h(0), op=ALU.mult),
                    waits=[me1])
                mA = self.dve(lambda v: v.tensor_tensor(
                    out=dst[:, lo, :], in0=dst[:, lo, :],
                    in1=b0_sb[:, :].unsqueeze(1).broadcast_to((128, 8, TT)), op=ALU.add),
                    waits=[me0])
                def poolmul(g):
                    for ic in range(8, 16):
                        mmp = g.tensor_tensor(out=dst[:, ic, :], in0=src_f32[:, ic, :],
                                              in1=b1_sb[:, :], op=ALU.mult)
                    return mmp
                mB = self.pool(poolmul, waits=[me1])
                def pooladd(g):
                    for ic in range(8, 16):
                        mmp = g.tensor_tensor(out=dst[:, ic, :], in0=dst[:, ic, :],
                                              in1=b0_sb[:, :], op=ALU.add)
                    return mmp
                mB = self.pool(pooladd, waits=[me0])
                if resid is None:
                    done = self.pool(lambda g: g.engine_nop(), waits=[mA, mB])
                    self._b1_free = done
                    return done
                rA = self.dve(lambda v: v.tensor_tensor(
                    out=resid[:, lo, :], in0=resid[:, lo, :], in1=dst[:, lo, :],
                    op=ALU.add), waits=[mA])
                rB = self.pool(lambda g: g.tensor_tensor(
                    out=resid[:, hi, :], in0=resid[:, hi, :], in1=dst[:, hi, :],
                    op=ALU.add), waits=[mB])
                done = self.pool(lambda g: g.engine_nop(), waits=[rA, rB])
                self._b1_free = done
                return done

            def emit_fm_proj(wname, out_cols, rhs_ready, evac_fn, extra_mm_waits=()):
                evs = []
                n_oc = out_cols // 128
                oc = 0
                while oc < n_oc:
                    cols = min(512, (n_oc - oc) * 128)
                    wv, wm, bidx = self.load_strip(wname, NIC, oc * 128, cols)
                    last_mm = None
                    for j in range(cols // 128):
                        bank_i = (oc + j) % 2
                        wl = [wm, rhs_ready] + list(extra_mm_waits)
                        def fn(t, wv=wv, j=j, bank_i=bank_i):
                            for ic in range(NIC):
                                mm = t.matmul(pb[bank_i][:, 0:512],
                                              wv[:, ic, j * 128:(j + 1) * 128],
                                              lnbuf[:, ic, :],
                                              start=(ic == 0), stop=(ic == NIC - 1))
                            return mm
                        mm_m = self.pe(fn, waits=wl, banks=(bank_i,))
                        last_mm = mm_m
                        evs.append(evac_fn(oc + j, bank_i, mm_m))
                    self.strip_free[bidx] = last_mm
                    oc += cols // 128
                return evs

            def emit_tokproj(wname, brow_idx, rhs_ready, evac_fn):
                evs = []
                for dc in range(4):
                    wv, wm, bidx = self.load_strip(wname, NIC, dc * 512, 512)
                    last_mm = None
                    for tb in range(NTB):
                        bank_i = (dc * NTB + tb) % 2
                        def fn(t, wv=wv, tb=tb, bank_i=bank_i, dc=dc):
                            t.matmul(pb[bank_i][:, 0:512], onesrow_bf[:],
                                     brow_bf[0:1, brow_idx * D + dc * 512:brow_idx * D + (dc + 1) * 512],
                                     start=True, stop=False)
                            for ic in range(NIC):
                                mm = t.matmul(pb[bank_i][:, 0:512],
                                              lnbuf[:, ic, tb * 128:(tb + 1) * 128],
                                              wv[:, ic, :], start=False, stop=(ic == NIC - 1))
                            return mm
                        mm_m = self.pe(fn, waits=[wm, rhs_ready, cm["orb"], cm["brw"]],
                                       banks=(bank_i,))
                        last_mm = mm_m
                        evs.append(evac_fn(tb, dc, bank_i, mm_m))
                    self.strip_free[bidx] = last_mm
                return evs

            def emit_scan(pre, kv_ready, g_ready, q_ready):
                dSbf4 = self.wstrip[:, 1, 7164:7164 + 4 * SC] \
                    .rearrange("p (a b) -> p a b", a=4)
                state_free = [self.psum_free[4 + j] for j in range(4)]
                num_free = [self.psum_free[j] for j in range(4)]
                sprev_last = None
                dve_last = None
                pe_last = None
                prev_mr = [None, None, None, None]
                prev_me = [None, None, None, None]
                GS = 4
                for hg in range(H // GS):
                    heads = [GS * hg + j for j in range(GS)]
                    mu = {}
                    ev = {}
                    nm = {}
                    for c in range(NCH):
                        tb, cpar = c // 2, (c % 2) * 64
                        for j, h in enumerate(heads):
                            sbank = pb[4 + j]
                            if c == 0:
                                wl = list(kv_ready) + [state_free[j]]
                            else:
                                wl = [ev[(j, c - 1)]]
                            mu[(j, c)] = self.pe(
                                lambda t, tb=tb, cp=cpar, h=h, c=c, sbank=sbank:
                                t.matmul(sbank[:, 0:SC],
                                         k2[cp:cp + 64, tb, h * 128:(h + 1) * 128],
                                         v_t[cp:cp + 64, tb, h, :],
                                         start=(c == 0), stop=True,
                                         skip_group_check=True), waits=wl)
                            pe_last = mu[(j, c)]
                        for j, h in enumerate(heads):
                            wde = [mu[(j, c)], nm.get((j, c - 1))]
                            ev[(j, c)] = self.dve(
                                lambda v, j=j, sbank=pb[4 + j]: v.tensor_copy(
                                    out=dSbf4[:, j, :], in_=sbank[:, 0:SC]), waits=wde)
                            if pre:
                                continue
                            nbi = j
                            def numfn(t, h=h, c=c, nbi=nbi, cp=cpar, j=j):
                                t.matmul(pb[nbi][cp:cp + 64, 0:SC],
                                         qbb[:, h, c * CS:(c + 1) * CS], dSbf4[:, j, :],
                                         start=True, stop=False)
                                return t.matmul(pb[nbi][cp:cp + 64, 0:SC],
                                                qbb[:, h, c * CS:(c + 1) * CS],
                                                Sprev_bf[:, h, :], start=False, stop=True)
                            wn = [ev[(j, c)], q_ready]
                            if c % 2 == 0:
                                wn.append(num_free[j])
                            nm[(j, c)] = self.pe(numfn, waits=wn)
                            pe_last = nm[(j, c)]
                        if pre or c % 2 == 0:
                            continue
                        tb2 = c // 2
                        for j, h in enumerate(heads):
                            nbi = j
                            md_ = self.dve(lambda v, nbi=nbi, j=j:
                                           v.tensor_scalar(
                                               out=den_s[:, j:j + 1],
                                               in0=pb[nbi][:, 128:129],
                                               scalar1=EPS_DEN, scalar2=None, op0=ALU.add),
                                           waits=[nm[(j, c)], prev_mr[j]])
                            mr_ = self.act(lambda s, j=j: (s.activation(
                                out=rd_s[:, j:j + 1], in_=den_s[:, j:j + 1], func=AF.Ln),
                                s.activation(out=rd_s[:, j:j + 1], in_=rd_s[:, j:j + 1],
                                             func=AF.Exp, scale=-1.0))[-1],
                                waits=[md_, prev_me[j]])
                            prev_mr[j] = mr_
                            def epi(v, nbi=nbi, tb2=tb2, h=h, j=j):
                                return v.scalar_tensor_tensor(
                                    out=ao_tok_t[:, tb2, h * 128:(h + 1) * 128],
                                    in0=pb[nbi][:, 0:128],
                                    scalar=rd_s[:, j:j + 1], op0=ALU.mult,
                                    in1=g_t[:, tb2, h * 128:(h + 1) * 128], op1=ALU.mult)
                            me_ = self.dve(epi, waits=[mr_] + list(g_ready),
                                           rd_banks=(nbi,))
                            prev_me[j] = me_
                            num_free[j] = me_
                            dve_last = me_
                    for j, h in enumerate(heads):
                        wfold = [mu[(j, NCH - 1)]]
                        if not pre:
                            wfold.append(nm[(j, NCH - 1)])
                        msp = self.dve(lambda v, h=h, j=j: v.tensor_tensor(
                            out=Sprev[:, h, :], in0=Sprev[:, h, :], in1=pb[4 + j][:, 0:SC],
                            op=ALU.add), waits=wfold, rd_banks=(4 + j,))
                        mspb = self.dve(lambda v, h=h: v.tensor_copy(
                            out=Sprev_bf[:, h, :], in_=Sprev[:, h, :]), waits=[msp])
                        state_free[j] = mspb
                        sprev_last = mspb
                        if pre:
                            dve_last = mspb
                if not pre:
                    tfree = [self.psum_free[0], self.psum_free[1]]
                    for h in range(H):
                        for tb2 in range(NTB):
                            tbi = (h * NTB + tb2) % 2
                            wt = [dve_last if h >= H - GS else prev_me[h % GS],
                                  cm["idb"], tfree[tbi]]
                            mt_ = self.pe(lambda t, h=h, tb2=tb2, tbi=tbi: t.transpose(
                                pb[tbi][:, 0:64].bitcast(BF16),
                                ao_tok_t[:, tb2, h * 128:(h + 1) * 128], ident_bf[:]),
                                waits=wt, banks=())
                            mte = self.dve(lambda v, h=h, tb2=tb2, tbi=tbi: v.tensor_copy(
                                out=ao_wr(h, tb2), in_=pb[tbi][:, 0:64].bitcast(BF16)),
                                waits=[mt_])
                            tfree[tbi] = mte
                            dve_last = mte
                            pe_last = mt_
                    self.psum_free[0] = tfree[0]
                    self.psum_free[1] = tfree[1]
                self.strip_free[0] = pe_last
                self.strip_free[1] = pe_last
                return sprev_last, dve_last

            st = dict(pe_end=None, dve_end=None, out_m=None, out_q=None, rope_free=None,
                      first_tile=True, sprev_ready=[mz, mzb])

            def emit_tile(ti, pre, x_src, kct, kst):
                t0 = ti * TT
                if pre:
                    mx = self.dma("sync", lambda: xb[:], lambda: x_src[:, :, t0:t0 + TT],
                                  self.s_x, waits=[st["dve_end"], st["out_m"]])
                    mx_q = None
                else:
                    mx_q = []
                    xsems = [self.s_x0, self.s_x1, self.s_x2, self.s_x3]
                    for qi in range(4):
                        sl = slice(4 * qi, 4 * qi + 4)
                        w = [st["dve_end"], st["out_q"][qi] if st["out_q"] else None]
                        mx_q.append(self.dma("sync",
                                             (lambda sl=sl: xb[:, sl, :]),
                                             (lambda sl=sl: x_src[:, sl, t0:t0 + TT]),
                                             xsems[qi], waits=w))
                    mx = mx_q[-1]
                self.dma("sync", lambda: kcos_s[:],
                         lambda: kct[:, ti * NTB:(ti + 1) * NTB, :], self.s_tab,
                         waits=[st["rope_free"], self.tabs_ready])
                mks = self.dma("sync", lambda: ksin_s[:],
                               lambda: kst[:, ti * NTB:(ti + 1) * NTB, :], self.s_tab)
                mkc = (self.s_tab, self.s_tab.val)
                if not pre:
                    self.dma("sync", lambda: cosq_s[:],
                             lambda: self.tab_c["cosq"][:, t0:t0 + TT], self.s_tab)
                    msq_ = self.dma("sync", lambda: sinq_s[:],
                                    lambda: self.tab_c["sinq"][:, t0:t0 + TT], self.s_tab)
                    mcq = (self.s_tab, self.s_tab.val)
                    mks = mcq

                if pre:
                    wstats = [mx, st["pe_end"], st["dve_end"]]
                    me0, me1 = emit_ln(xb[:, :, :], sqscr_k, wstats)
                else:
                    wstats = [st["pe_end"], st["dve_end"]]
                    me0, me1 = emit_ln(xb[:, :, :], sqscr_k, wstats, quarters=mx_q)
                ln_ready = emit_norm(lnbuf, xb, me0, me1)
                if not pre:
                    self.tap("ln0", lambda: lnbuf[:], [ln_ready])
                else:
                    self.tap("pln0", lambda: lnbuf[:], [ln_ready])

                def k_write(tb, dc, bank_i, mm_m):
                    return self.dve(lambda v, tb=tb, dc=dc, b=bank_i: v.tensor_copy(
                        out=ktile[:, tb, dc * 512:(dc + 1) * 512], in_=pb[b][:, 0:512]),
                        waits=[mm_m], rd_banks=(bank_i,))
                k_evs = emit_tokproj("wk", 0, ln_ready, k_write)
                if not pre:
                    self.tap("kraw", lambda: ktile[:], k_evs[-1:])
                kcr = kcos_s[:, :, :].unsqueeze(2).broadcast_to((128, NTB, H, 64))
                ksr = ksin_s[:, :, :].unsqueeze(2).broadcast_to((128, NTB, H, 64))
                def iv(ap, par):
                    return ap.rearrange("p a (h f two) -> p a h f two", h=H, two=2)[:, :, :, :, par]
                kev, kod = iv(ktile, 0), iv(ktile, 1)
                k2ev, k2od = iv(k2, 0), iv(k2, 1)
                scr = iv(g_t, 0)
                mr1 = self.dve(lambda v: v.tensor_tensor(out=k2ev, in0=kev, in1=kcr, op=ALU.mult),
                               waits=k_evs + [mkc, mks])
                mr2 = self.dve(lambda v: v.tensor_tensor(out=k2od, in0=kod, in1=kcr, op=ALU.mult))
                mr3 = self.dve(lambda v: v.tensor_tensor(out=scr, in0=kod, in1=ksr, op=ALU.mult))
                mr4 = self.dve(lambda v: v.tensor_tensor(out=k2ev, in0=k2ev, in1=scr,
                                                         op=ALU.subtract))
                mr5 = self.dve(lambda v: v.tensor_tensor(out=scr, in0=kev, in1=ksr, op=ALU.mult))
                mr6 = self.dve(lambda v: v.tensor_tensor(out=k2od, in0=k2od, in1=scr, op=ALU.add))
                if not pre:
                    self.tap("krope", lambda: k2[:], [mr6])
                mr7 = self.dve(lambda v: v.tensor_scalar(out=ktile, in0=k2, scalar1=0.0,
                                                         scalar2=None, op0=ALU.min), waits=[mr6])
                mr8 = self.act(lambda s: s.activation(out=ktile, in_=ktile, func=AF.Exp),
                               waits=[mr7])
                k_done = self.dve(lambda v: v.scalar_tensor_tensor(
                    out=k2, in0=k2, scalar=0.0, op0=ALU.max, in1=ktile, op1=ALU.add),
                    waits=[mr8])
                rope_free_k = mr5

                mones = self.dve(lambda v: v.memset(v_t[:, :, :, 128:129], 1.0),
                                 waits=[st["pe_end"], st["dve_end"]])
                def v_write(tb, dc, bank_i, mm_m):
                    return self.dve(lambda v, tb=tb, dc=dc, b=bank_i: v.tensor_copy(
                        out=v_t[:, tb, dc * 4:(dc + 1) * 4, 0:128],
                        in_=pb[b][:, 0:512].rearrange("p (a c) -> p a c", a=4)),
                        waits=[mm_m], rd_banks=(bank_i,))
                v_evs = emit_tokproj("wv", 1, ln_ready, v_write)

                g_evs, q_done, rope_free_q = [], None, None
                if not pre:
                    def g_write(tb, dc, bank_i, mm_m):
                        return self.act(lambda s, tb=tb, dc=dc, b=bank_i: s.activation(
                            out=g_t[:, tb, dc * 512:(dc + 1) * 512], in_=pb[b][:, 0:512],
                            func=AF.Sigmoid), waits=[mm_m, mr5], rd_banks=(bank_i,))
                    g_evs = emit_tokproj("wg", 2, ln_ready, g_write)

                    q_evs = []
                    def q_write(oc, bank_i, mm_m):
                        return self.dve(lambda v, oc=oc, b=bank_i: v.tensor_scalar(
                            out=qbb[:, oc, :], in0=pb[b][:, 0:512], scalar1=bq_s[:, oc:oc + 1],
                            scalar2=None, op0=ALU.add), waits=[mm_m, cm["bq"]],
                            rd_banks=(bank_i,))
                    q_evs = emit_fm_proj("wq", D, ln_ready, q_write)
                    q_head_done = []
                    for h in range(H):
                        sbi2 = 2 + h % 2
                        wq1 = [q_evs[h], mcq, msq_]
                        mqc = self.dve(lambda v, h=h: v.tensor_tensor(
                            out=qhtmp[:, 0, :], in0=qbb[:, h, :], in1=cosq_s[:], op=ALU.mult),
                            waits=wq1)
                        msw = self.pe(lambda t, h=h, sbi2=sbi2: t.matmul(
                            pb[sbi2][:, 0:512], pswap_bf[:], qbb[:, h, :], start=True, stop=True),
                            waits=[q_evs[h], cm["psw"]], banks=(sbi2,))
                        msin = self.dve(lambda v, sbi2=sbi2: v.tensor_tensor(
                            out=qhtmp[:, 1, :], in0=pb[sbi2][:, 0:512], in1=sinq_s[:],
                            op=ALU.mult), waits=[msw], rd_banks=(sbi2,))
                        mqr = self.dve(lambda v, h=h: v.tensor_tensor(
                            out=qbb[:, h, :], in0=qhtmp[:, 0, :], in1=qhtmp[:, 1, :], op=ALU.add),
                            waits=[msin, mqc])
                        mmin = self.dve(lambda v, h=h: v.tensor_scalar(
                            out=qhtmp[:, 0, :], in0=qbb[:, h, :], scalar1=0.0, scalar2=None,
                            op0=ALU.min))
                        mex = self.act(lambda s: s.activation(out=qhtmp[:, 1, :],
                                                              in_=qhtmp[:, 0, :], func=AF.Exp),
                                       waits=[mmin])
                        mfin = self.dve(lambda v, h=h: v.scalar_tensor_tensor(
                            out=qbb[:, h, :], in0=qbb[:, h, :], scalar=0.0, op0=ALU.max,
                            in1=qhtmp[:, 1, :], op1=ALU.add), waits=[mex])
                        q_head_done.append(mfin)
                    q_done = q_head_done[-1]
                    rope_free_q = q_head_done[-1]

                if pre:
                    self.tap("pk", lambda: k2[:], [k_done])
                    self.tap("pv", lambda: v_t[:], [v_evs[-1], mones])
                if not pre:
                    self.tap("kf", lambda: k2[:], [k_done])
                    self.tap("vt", lambda: v_t[:], [v_evs[-1], mones])
                    if g_evs:
                        self.tap("gt", lambda: g_t[:], [g_evs[-1]])
                    if q_done:
                        self.tap("qf", lambda: qbb[:], [q_done])
                kv_ready = [k_done, v_evs[-1], mones]
                g_ready = [g_evs[-1]] if g_evs else []
                sprev_last, scan_dve_last = emit_scan(pre, kv_ready, g_ready, q_done)
                st["rope_free"] = rope_free_q or rope_free_k
                if not pre:
                    self.tap("sprev", lambda: Sprev[:], [sprev_last])
                    self.tap("ao", lambda: k2[:], [scan_dve_last])

                if pre:
                    self.tap("pscan", lambda: Sprev[:], [sprev_last])
                    st["pe_end"] = None
                    st["dve_end"] = scan_dve_last
                    st["sprev_ready"] = [sprev_last]
                    return

                def o_write(oc, bank_i, mm_m):
                    return self.dve(lambda v, oc=oc, b=bank_i: v.tensor_scalar(
                        out=o_f32[:, oc, :], in0=pb[b][:, 0:512], scalar1=bo_s[:, oc:oc + 1],
                        scalar2=None, op0=ALU.add), waits=[mm_m, cm["bo"]], rd_banks=(bank_i,))
                def emit_wo():
                    evs = []
                    n_oc = NIC
                    oc = 0
                    while oc < n_oc:
                        wv, wm, bidx = self.load_strip("wo", NIC, oc * 128, 512)
                        last_mm = None
                        for j in range(4):
                            bank_i = (oc + j) % 2
                            def fn(t, wv=wv, j=j, bank_i=bank_i):
                                for ic in range(NIC):
                                    mm = t.matmul(pb[bank_i][:, 0:512],
                                                  wv[:, ic, j * 128:(j + 1) * 128],
                                                  ao_rd(ic),
                                                  start=(ic == 0), stop=(ic == NIC - 1))
                                return mm
                            mm_m = self.pe(fn, waits=[wm, scan_dve_last], banks=(bank_i,))
                            last_mm = mm_m
                            evs.append(o_write(oc + j, bank_i, mm_m))
                        self.strip_free[bidx] = last_mm
                        oc += 4
                    return evs
                o_evs = emit_wo()

                self.tap("o", lambda: o_f32, [o_evs[-1]])
                me0, me1 = emit_ln(o_f32, sqscr_g, [],
                                   quarters=[o_evs[3], o_evs[7], o_evs[11], o_evs[15]])
                mx1 = emit_norm(lnbuf, o_f32, me0, me1, resid=xb)

                self.tap("x1", lambda: xb[:], [mx1])
                me0b, me1b = emit_ln(xb[:, :, :], sqscr_g, [mx1])
                ln1_ready = emit_norm(lnbuf, xb, me0b, me1b)

                self.tap("ln1", lambda: lnbuf[:], [ln1_ready])
                def gatefn(t):
                    for ic in range(NIC):
                        mm = t.matmul(pb[2][0:8, 0:TT], gate_bf[:, ic, :], lnbuf[:, ic, :],
                                      start=(ic == 0), stop=(ic == NIC - 1))
                    return mm
                mgate = self.pe(gatefn, waits=[ln1_ready, cm["gt"]], banks=(2,))
                msig = self.act(lambda s: s.activation(out=sfm[:], in_=pb[2][0:8, 0:TT],
                                                       func=AF.Sigmoid), waits=[mgate],
                                rd_banks=(2,))
                tr_free_r = None
                mev = None
                for tb in range(NTB):
                    wt = [msig, cm["idf"]]
                    if tr_free_r is not None:
                        wt.append(tr_free_r)
                    mtr = self.pe(lambda t, tb=tb: t.transpose(
                        pb[3][0:128, 0:8], sfm[0:8, tb * 128:(tb + 1) * 128], ident_f[0:8, 0:8]),
                        waits=wt, banks=(3,))
                    mev = self.dve(lambda v, tb=tb: v.tensor_copy(out=stok[:, tb, :],
                                                                  in_=pb[3][0:128, 0:8]),
                                   waits=[mtr], rd_banks=(3,))
                    tr_free_r = mev
                mpc = self.pool(lambda g: g.tensor_copy(out=stok2[:], in_=stok[:]),
                                waits=[mev])
                m1_ = self.dve(lambda v: [v.tensor_tensor(
                    out=r_t[:, tb, :], in0=stok2[:, tb, :], in1=gbias_s[:], op=ALU.add)
                    for tb in range(NTB)][-1], waits=[mpc, cm["gb"]])
                m2_ = self.dve(lambda v: v.tensor_reduce(
                    out=r_m1[:], in_=r_t[:], axis=mybir.AxisListType.X, op=ALU.max),
                    waits=[m1_])
                m2b = self.pool(lambda g: g.tensor_copy(out=r_m1b[:], in_=r_m1[:]),
                                waits=[m2_])
                m3_ = self.dve(lambda v: [v.tensor_scalar(
                    out=r_w[:, tb, :], in0=r_t[:, tb, :], scalar1=r_m1b[:, tb:tb + 1],
                    scalar2=None, op0=ALU.is_ge) for tb in range(NTB)][-1], waits=[m2b])
                m4_ = self.dve(lambda v: [v.scalar_tensor_tensor(
                    out=r_w[:, tb, :], in0=r_w[:, tb, :], scalar=-1e30, op0=ALU.mult,
                    in1=r_t[:, tb, :], op1=ALU.add) for tb in range(NTB)][-1], waits=[m3_])
                m5_ = self.dve(lambda v: v.tensor_reduce(
                    out=r_m1[:], in_=r_w[:], axis=mybir.AxisListType.X, op=ALU.max),
                    waits=[m4_])
                m5b = self.pool(lambda g: g.tensor_copy(out=r_m1b[:], in_=r_m1[:]),
                                waits=[m5_])
                m6_ = self.dve(lambda v: [v.tensor_scalar(
                    out=r_w[:, tb, :], in0=r_t[:, tb, :], scalar1=r_m1b[:, tb:tb + 1],
                    scalar2=None, op0=ALU.is_ge) for tb in range(NTB)][-1], waits=[m5b])
                m7_ = self.dve(lambda v: [v.tensor_tensor(
                    out=r_w[:, tb, :], in0=r_w[:, tb, :], in1=stok2[:, tb, :], op=ALU.mult)
                    for tb in range(NTB)][-1], waits=[m6_])
                m8_ = self.dve(lambda v: v.tensor_reduce(
                    out=r_ws4[:], in_=r_w[:], axis=mybir.AxisListType.X, op=ALU.add),
                    waits=[m7_])
                mrr = self.act(lambda s: (s.activation(out=r_ws4[:], in_=r_ws4[:], func=AF.Ln),
                                          s.activation(out=r_ws4[:], in_=r_ws4[:], func=AF.Exp,
                                                       scale=-1.0))[-1], waits=[m8_])
                route_done = None
                tr_free_w = None
                for tb in range(NTB):
                    mwe_ = self.dve(lambda v, tb=tb: v.tensor_scalar(
                        out=stok[:, tb, :], in0=r_w[:, tb, :], scalar1=r_ws4[:, tb:tb + 1],
                        scalar2=None, op0=ALU.mult), waits=[mrr])
                    mtw = self.pe(lambda t, tb=tb: t.transpose(
                        pb[3][0:8, 0:128], stok[:, tb, :], ident_f[:]),
                        waits=[mwe_] + ([tr_free_w] if tr_free_w else []), banks=(3,))
                    mevw = self.dve(lambda v, tb=tb: v.tensor_copy(
                        out=weT[0:8, tb * 128:(tb + 1) * 128], in_=pb[3][0:8, 0:128]),
                        waits=[mtw], rd_banks=(3,))
                    tr_free_w = mevw
                    route_done = mevw
                self.tap("sfm", lambda: sfm[:], [msig])
                self.tap("we", lambda: stok[:], [route_done])
                uw_guard = [None, None]

                def sh_evac_half(p):
                    def evac(j, oc, bank_i, mm_m):
                        if j < 11:
                            return self.dve(lambda v, j=j, b=bank_i: v.tensor_copy(
                                out=h_shh[:, j, :], in_=pb[b][:, 0:512]), waits=[mm_m],
                                rd_banks=(bank_i,))
                        sl = j % 2
                        ms_ = self.act(lambda s, b=bank_i, sl=sl: s.activation(
                            out=uw_t[:, sl, :], in_=pb[b][:, 0:512], func=AF.Silu),
                            waits=[mm_m, uw_guard[sl]], rd_banks=(bank_i,))
                        mr = self.dve(lambda v, j=j, sl=sl: v.tensor_tensor(
                            out=h_shh[:, j - 11, :], in0=h_shh[:, j - 11, :],
                            in1=uw_t[:, sl, :], op=ALU.mult), waits=[ms_])
                        uw_guard[sl] = mr
                        return mr
                    return evac

                def make_e_evac(eidx, hbuf, we_ready):
                    def evac(j, oc, bank_i, mm_m):
                        if j < 11:
                            m = self.act(lambda s, j=j, b=bank_i: s.activation(
                                out=sg_e[:, j, :], in_=pb[b][:, 0:512], func=AF.Silu),
                                waits=[mm_m, self._sg_guard], rd_banks=(bank_i,))
                            return m
                        sl = j % 2
                        mu = self.dve(lambda v, b=bank_i, e=eidx, sl=sl: v.tensor_tensor(
                            out=uw_t[:, sl, :], in0=pb[b][:, 0:512], in1=we2[:, e % 2, :],
                            op=ALU.mult), waits=[mm_m, we_ready, uw_guard[sl]],
                            rd_banks=(bank_i,))
                        mh = self.dve(lambda v, j=j, hb=hbuf, sl=sl: v.tensor_tensor(
                            out=hb[:, j - 11, :], in0=sg_e[:, j - 11, :], in1=uw_t[:, sl, :],
                            op=ALU.mult), waits=[mu])
                        uw_guard[sl] = mh
                        if j == 21:
                            self._sg_guard = mh
                        return mh
                    return evac

                def emit_cols_proj(wname, oc_list, rhs_ready, evac_fn, extra=()):
                    evs = []
                    pos = 0
                    while pos < len(oc_list):
                        run = [oc_list[pos]]
                        while len(run) < 4 and pos + len(run) < len(oc_list) and \
                                oc_list[pos + len(run)] == run[-1] + 1:
                            run.append(oc_list[pos + len(run)])
                        wv, wm, bidx = self.load_strip(wname, NIC, run[0] * 128, len(run) * 128)
                        last_mm = None
                        for j, oc in enumerate(run):
                            bank_i = (pos + j) % 2
                            wl = [wm, rhs_ready] + list(extra)
                            def fn(t, wv=wv, j=j, bank_i=bank_i):
                                for ic in range(NIC):
                                    mm = t.matmul(pb[bank_i][:, 0:512],
                                                  wv[:, ic, j * 128:(j + 1) * 128],
                                                  lnbuf[:, ic, :],
                                                  start=(ic == 0), stop=(ic == NIC - 1))
                                return mm
                            mm_m = self.pe(fn, waits=wl, banks=(bank_i,))
                            last_mm = mm_m
                            evs.append(evac_fn(pos + j, oc, bank_i, mm_m))
                        self.strip_free[bidx] = last_mm
                        pos += len(run)
                    return evs

                groups = [("wdnA", 0, [(0, h_e0), (1, h_e1)]),
                          ("wdnB", 1, [(2, h_e0), (3, h_e1)]),
                          ("wdnC", None, [(4, h_e0), (5, h_e1)]),
                          ("wdnD", None, [(6, h_e0), (7, h_e1)])]
                f_last = None
                f_final = [None] * NIC
                hbuf_free = {}
                for gi, (dnname, shpass, emems) in enumerate(groups):
                    we_ready = None
                    for eidx, _hb in emems:
                        bi = 2 + eidx % 2
                        mwe = self.pe(lambda t, e=eidx, bi=bi: t.matmul(
                            pb[bi][:, 0:TT], sel8_bf[:, e * 128:(e + 1) * 128], weT[0:8, :],
                            start=True, stop=True), waits=[route_done, cm["sel8"]],
                            banks=(bi,))
                        we_ready = self.dve(lambda v, e=eidx, bi=bi: v.tensor_copy(
                            out=we2[:, e % 2, :], in_=pb[bi][:, 0:TT]),
                            waits=[mwe, hbuf_free.get("we")], rd_banks=(bi,))
                    h_ready = []
                    if shpass is not None:
                        ocs = list(range(shpass * 11, shpass * 11 + 11)) + \
                              list(range(22 + shpass * 11, 22 + shpass * 11 + 11))
                        evs = emit_cols_proj("wsgw", ocs, ln1_ready, sh_evac_half(shpass),
                                             extra=[hbuf_free.get(id(h_shh))])
                        h_ready.append(evs[-1])
                    for eidx, hbuf in emems:
                        evs = emit_cols_proj(f"wup{eidx}", list(range(22)), ln1_ready,
                                             make_e_evac(eidx, hbuf, we_ready),
                                             extra=[hbuf_free.get(id(hbuf))])
                        h_ready.append(evs[-1])
                    kc = 33 if shpass is not None else 22
                    rows = ([(h_shh, 0, 11)] if shpass is not None else []) + \
                           [(emems[0][1], kc - 22, 11), (emems[1][1], kc - 11, 11)]
                    down_last = None
                    for oc in range(NIC):
                        wv, wm, bidx = self.load_strip(dnname, kc, oc * 128, 128)
                        bank_i = oc % 2
                        def fn(t, wv=wv, rows=rows, bank_i=bank_i):
                            n_tot = sum(nr for _, _, nr in rows)
                            i = 0
                            for hb, r0, nr in rows:
                                for r in range(nr):
                                    mm = t.matmul(pb[bank_i][:, 0:512], wv[:, r0 + r, :],
                                                  hb[:, r, :], start=(i == 0),
                                                  stop=(i == n_tot - 1))
                                    i += 1
                            return mm
                        mm_m = self.pe(fn, waits=[wm] + h_ready, banks=(bank_i,))
                        down_last = mm_m
                        self.strip_free[bidx] = mm_m
                        if gi == 0:
                            f_last = self.dve(lambda v, oc=oc, b=bank_i: v.tensor_copy(
                                out=f_sb[:, oc, :], in_=pb[b][:, 0:512]), waits=[mm_m],
                                rd_banks=(bank_i,))
                        else:
                            f_last = self.dve(lambda v, oc=oc, b=bank_i: v.tensor_tensor(
                                out=f_sb[:, oc, :], in0=f_sb[:, oc, :], in1=pb[b][:, 0:512],
                                op=ALU.add), waits=[mm_m], rd_banks=(bank_i,))
                        f_final[oc] = f_last
                    for eidx, hbuf in emems:
                        hbuf_free[id(hbuf)] = f_last
                    hbuf_free[id(h_shh)] = f_last
                    hbuf_free["we"] = f_last

                self.tap("fsb", lambda: f_sb, [f_last])
                me0c, me1c = emit_ln(f_sb, sqscr_ffn, [],
                                     quarters=[f_final[3], f_final[7], f_final[11],
                                               f_final[15]], sq_gate=f_final[15])
                out_q = []
                n3s = []
                osems = [self.s_o0, self.s_o1, self.s_o2, self.s_o3]
                for qi in range(4):
                    sl = slice(4 * qi, 4 * qi + 4)
                    b1r = b1_sb[:, :].unsqueeze(1).broadcast_to((128, 4, TT))
                    b0r = b0_sb[:, :].unsqueeze(1).broadcast_to((128, 4, TT))
                    if qi % 2 == 0:
                        n1 = self.dve(lambda e, sl=sl, b1r=b1r: e.tensor_tensor(
                            out=lnbuf[:, sl, :], in0=f_sb[:, sl, :], in1=b1r, op=ALU.mult),
                            waits=[me1c])
                        n2 = self.dve(lambda e, sl=sl, b0r=b0r: e.tensor_tensor(
                            out=lnbuf[:, sl, :], in0=lnbuf[:, sl, :], in1=b0r, op=ALU.add),
                            waits=[me0c])
                        n3 = self.dve(lambda e, sl=sl: e.tensor_tensor(
                            out=xb[:, sl, :], in0=xb[:, sl, :], in1=lnbuf[:, sl, :],
                            op=ALU.add), waits=[n2])
                    else:
                        def pm(g, qi=qi):
                            for ic in range(4 * qi, 4 * qi + 4):
                                mmp = g.tensor_tensor(out=lnbuf[:, ic, :],
                                                      in0=f_sb[:, ic, :], in1=b1_sb[:, :],
                                                      op=ALU.mult)
                            return mmp
                        n1 = self.pool(pm, waits=[me1c])
                        def pa(g, qi=qi):
                            for ic in range(4 * qi, 4 * qi + 4):
                                mmp = g.tensor_tensor(out=lnbuf[:, ic, :],
                                                      in0=lnbuf[:, ic, :], in1=b0_sb[:, :],
                                                      op=ALU.add)
                            return mmp
                        n2 = self.pool(pa, waits=[me0c])
                        def px(g, qi=qi):
                            for ic in range(4 * qi, 4 * qi + 4):
                                mmp = g.tensor_tensor(out=xb[:, ic, :], in0=xb[:, ic, :],
                                                      in1=lnbuf[:, ic, :], op=ALU.add)
                            return mmp
                        n3 = self.pool(px, waits=[n2])
                    n3s.append(n3)
                    out_q.append(self.dma("gpsimd",
                                          (lambda sl=sl, t0=t0: self.y_out[:, sl, t0:t0 + TT]),
                                          (lambda sl=sl: xb[:, sl, :]),
                                          osems[qi], waits=[n3]))
                st["out_m"] = out_q[-1]
                st["out_q"] = out_q
                st["pe_end"] = down_last
                st["dve_end"] = self.pool(lambda g: g.engine_nop(), waits=n3s)
                self._b1_free = st["dve_end"]
                self.tap("x2", lambda: xb[:], [st["dve_end"]])

            try:
                for ti in range(n_pre):
                    emit_tile(ti, True, self.x_pre, self.tab_c["kcp"], self.tab_c["ksp"])
                mmask = self.dve(lambda v: v.tensor_scalar(
                    out=Sprev.rearrange("p a b -> p (a b)"),
                    in0=Sprev.rearrange("p a b -> p (a b)"), scalar1=maskS_s[:], scalar2=None,
                    op0=ALU.mult), waits=st["sprev_ready"] + [cm["mk"]])
                mmaskb = self.dve(lambda v: v.tensor_copy(out=Sprev_bf[:], in_=Sprev[:]),
                                  waits=[mmask])
                st["sprev_ready"] = [mmaskb]
                st["dve_end"] = mmaskb

                for ti in range(n_tiles):
                    emit_tile(ti, False, self.x_own, self.tab_c["kco"], self.tab_c["kso"])
            except StopEmission:
                pass


            for snm in ("s_out", "s_o0", "s_o1", "s_o2", "s_o3"):
                tot = getattr(self, snm).val
                if tot:
                    self.items["gpsimd"].append(((), lambda g, snm=snm, tot=tot: g.wait_ge(
                        handles[snm], tot), None, 0))

            def emitter(engname):
                def run(eng):
                    for waits, fn, inc, inc_n in self.items[engname]:
                        for sem, val in waits:
                            eng.wait_ge(handles[sem.name], val)
                        inst = fn(eng)
                        if inc is not None:
                            inst.then_inc(handles[inc.name], inc_n)
                return run
            block.sync(emitter("sync"))
            block.gpsimd(emitter("gpsimd"))
            block.vector(emitter("vector"))
            block.scalar(emitter("scalar"))
            block.tensor(emitter("tensor"))

        return nc


def _fm(a2d, kc):
    k, m = a2d.shape
    assert k == kc * 128
    return np.ascontiguousarray(a2d.reshape(kc, 128, m).transpose(1, 0, 2), dtype=np.float32)


def make_core_inputs(full, core, n_tiles=NT_FULL, n_pre=NT_FULL):
    b, half = core // 2, core % 2
    ntok, npre = n_tiles * TT, max(n_pre, 1) * TT if n_pre else TT
    npre_real = n_pre * TT
    x = np.asarray(full["L0_attn_x"], dtype=np.float32)
    own0 = half * NTOK
    d = {}
    d["x_own"] = _fm(np.ascontiguousarray(x[b, own0:own0 + ntok, :].T), NIC)
    if half == 1 and n_pre:
        x_pre = x[b, NTOK - npre_real:NTOK, :]
        pre0 = NTOK - npre_real
    else:
        x_pre = np.zeros((npre_real if n_pre else TT, D), np.float32)
        pre0 = 0
    xp = np.zeros((D, max(npre_real, TT)), np.float32)
    if n_pre:
        xp[:, :npre_real] = x_pre.T
    d["x_pre"] = _fm(xp, NIC)
    d["wq"] = _fm(np.asarray(full["L0_attn_Wq"]).T, NIC)
    d["wk"] = _fm(np.asarray(full["L0_attn_Wk"]).T, NIC)
    d["wv"] = _fm(np.asarray(full["L0_attn_Wv"]).T, NIC)
    d["wg"] = _fm(np.asarray(full["L0_attn_Wg"]).T, NIC)
    d["wo"] = _fm(np.asarray(full["L0_attn_Wo"]).T, NIC)
    d["wgate"] = _fm(np.asarray(full["L0_ffn_gate_w"]).T, NIC)
    d["wsgw"] = _fm(np.asarray(full["L0_ffn_shared_gw"]).T, NIC)
    up = np.asarray(full["L0_ffn_up_w"])
    dn = np.asarray(full["L0_ffn_down_w"])
    for e in range(NE):
        d[f"wup{e}"] = _fm(up[e].T, NIC)
    sdwT = np.asarray(full["L0_ffn_shared_dw"]).T
    dnT = [dn[e].T for e in range(NE)]
    d["wdnA"] = _fm(np.concatenate([sdwT[:11 * 128], dnT[0], dnT[1]], 0), 33)
    d["wdnB"] = _fm(np.concatenate([sdwT[11 * 128:], dnT[2], dnT[3]], 0), 33)
    d["wdnC"] = _fm(np.concatenate([dnT[4], dnT[5]], 0), 22)
    d["wdnD"] = _fm(np.concatenate([dnT[6], dnT[7]], 0), 22)
    d["bq"] = np.ascontiguousarray(np.asarray(full["L0_attn_bq"], np.float32).reshape(NIC, 128).T)
    d["bo"] = np.ascontiguousarray(np.asarray(full["L0_attn_bo"], np.float32).reshape(NIC, 128).T)
    d["brows"] = np.concatenate(
        [np.asarray(full["L0_attn_bk"]), np.asarray(full["L0_attn_bv"]),
         np.asarray(full["L0_attn_bg"])]).astype(np.float32).reshape(1, 3 * D)
    d["gbias"] = np.ascontiguousarray(
        np.broadcast_to(np.asarray(full["L0_ffn_gate_bias"], np.float32), (128, NE)))
    d["maskS"] = np.full((128, 1), 1.0 if (half == 1 and n_pre) else 0.0, np.float32)
    inv = 1.0 / 10000.0 ** (np.arange(0, DK, 2, dtype=np.float64) / DK)
    pos_own = np.arange(own0, own0 + ntok, dtype=np.float64)
    ang = pos_own[None, :] * inv[:, None]
    cq = np.empty((128, ntok), np.float32)
    sq = np.empty((128, ntok), np.float32)
    cq[0::2] = np.cos(ang); cq[1::2] = np.cos(ang)
    sq[0::2] = -np.sin(ang); sq[1::2] = np.sin(ang)
    d["cosq"], d["sinq"] = np.ascontiguousarray(cq), np.ascontiguousarray(sq)

    def ktab(p0, n):
        pos = np.arange(p0, p0 + n, dtype=np.float64)
        a = pos[:, None] * inv[None, :]
        c = np.cos(a).reshape(n // 128, 128, 64).transpose(1, 0, 2)
        s = np.sin(a).reshape(n // 128, 128, 64).transpose(1, 0, 2)
        return (np.ascontiguousarray(c, np.float32), np.ascontiguousarray(s, np.float32))
    d["kcos_o"], d["ksin_o"] = ktab(own0, ntok)
    d["kcos_p"], d["ksin_p"] = ktab(pre0, max(npre_real, TT))
    d["ident"] = np.eye(128, dtype=np.float32)
    psw = np.zeros((128, 128), np.float32)
    for f in range(64):
        psw[2 * f, 2 * f + 1] = 1.0
        psw[2 * f + 1, 2 * f] = 1.0
    d["pswap"] = psw
    d["onescol"] = np.ones((128, 1), np.float32)
    d["onesrow"] = np.ones((1, 128), np.float32)
    sel8 = np.zeros((8, 8 * 128), np.float32)
    for e in range(NE):
        sel8[e, e * 128:(e + 1) * 128] = 1.0
    d["sel8"] = sel8
    return d


_prog_cache = {}


def get_program(n_tiles=NT_FULL, n_pre=NT_FULL, debug_taps=()):
    key = (n_tiles, n_pre, tuple(debug_taps))
    if key not in _prog_cache:
        _prog_cache[key] = Builder(n_tiles, n_pre, debug_taps).build()
    return _prog_cache[key]


def kernel(**inputs):
    nc = get_program()
    in_maps = [make_core_inputs(inputs, c) for c in range(8)]
    res = run_bass_kernel_spmd(nc, in_maps, core_ids=list(range(8)))
    out = np.empty((B, T, D), np.float32)
    for c in range(8):
        b, half = c // 2, c % 2
        y = res.results[c]["y"]
        y2 = y.transpose(1, 0, 2).reshape(D, NTOK)
        out[b, half * NTOK:(half + 1) * NTOK, :] = y2.T
    return out
